# version 02 (final)
# baseline (speedup 1.0000x reference)
import math
import numpy as np
import concourse.bass as bass
import concourse.mybir as mybir
from concourse.bass_utils import run_bass_kernel_spmd
from contextlib import ExitStack

F32 = mybir.dt.float32
BF16 = mybir.dt.bfloat16
AF = mybir.ActivationFunctionType
ALU = mybir.AluOpType
AX = mybir.AxisListType

NCORES = 8
T = 1024
TS = 512
NSTRIP = T // TS
D = 2048
KC = D // 128
NH0, HD = 16, 128
INTER = 5632
IGRP, NIG = 11, 4
ROT = 32


class Ctx:

    def __init__(self, nc, es, blk):
        self.nc, self.blk = nc, blk
        self.sem = {}
        self.cnt = {}
        for name in ("ld", "st", "te", "ve", "se", "ge"):
            self.sem[name] = es.enter_context(nc.semaphore(name))
            self.cnt[name] = 0

    def _pre(self, eng, key, extra):
        waits = {k: self.cnt[k] for k in ("te", "ve", "se")}
        waits[key] = self.cnt[key]
        if extra:
            waits.update(extra)
        for k, v in waits.items():
            if k == key:
                continue
            if v > 0:
                eng.wait_ge(self.sem[k], v)
        if self.cnt[key] > 0:
            eng.wait_ge(self.sem[key], self.cnt[key])

    def te(self, fn, **extra):
        @self.blk.tensor
        def _(eng):
            self._pre(eng, "te", extra)
            ins = fn(eng)
            self.i("te", ins)

    def ve(self, fn, **extra):
        @self.blk.vector
        def _(eng):
            self._pre(eng, "ve", extra)
            ins = fn(eng)
            self.i("ve", ins)

    def se(self, fn, **extra):
        @self.blk.scalar
        def _(eng):
            self._pre(eng, "se", extra)
            ins = fn(eng)
            self.i("se", ins)

    def ge_dma(self, fn, **extra):
        @self.blk.gpsimd
        def _(eng):
            self._pre(eng, "ge", extra)
            ins = fn(eng)
            self.i("st", ins, 16)

    def sy_dma(self, fn, **extra):
        @self.blk.sync
        def _(eng):
            for k, v in (extra or {}).items():
                if v > 0:
                    eng.wait_ge(self.sem[k], v)
            for ins in fn(eng):
                self.i("ld", ins, 16)

    def i(self, key, instr, n=1):
        instr.then_inc(self.sem[key], n)
        self.cnt[key] += n
        return self.cnt[key]


def build_l0():
    nc = bass.Bass("TRN2", target_bir_lowering=False, debug=False)
    x0_d = nc.dram_tensor("x0", [D, T], F32, kind="ExternalInput").ap()
    wqkv_d = nc.dram_tensor("wqkv", [D, 3 * D], F32, kind="ExternalInput").ap()
    bqkv_d = nc.dram_tensor("bqkv", [3 * D], F32, kind="ExternalInput").ap()
    gwT_d = nc.dram_tensor("gwT", [D, INTER], F32, kind="ExternalInput").ap()
    uwT_d = nc.dram_tensor("uwT", [D, INTER], F32, kind="ExternalInput").ap()
    dwT_d = nc.dram_tensor("dwT", [INTER, D], F32, kind="ExternalInput").ap()
    gs_d = nc.dram_tensor("gs", [INTER], F32, kind="ExternalInput").ap()
    us_d = nc.dram_tensor("us", [INTER], F32, kind="ExternalInput").ap()
    ds_d = nc.dram_tensor("ds", [D], F32, kind="ExternalInput").ap()
    cos_d = nc.dram_tensor("cosx", [ROT, T], F32, kind="ExternalInput").ap()
    sin_d = nc.dram_tensor("sinx", [ROT, T], F32, kind="ExternalInput").ap()
    p32_d = nc.dram_tensor("p32", [ROT, ROT], F32, kind="ExternalInput").ap()
    eye_d = nc.dram_tensor("eye", [128, 128], F32, kind="ExternalInput").ap()

    q_o = nc.dram_tensor("q_o", [D, T], F32, kind="ExternalOutput").ap()
    k_o = nc.dram_tensor("k_o", [D, T], F32, kind="ExternalOutput").ap()
    v_o = nc.dram_tensor("v_o", [T, D], F32, kind="ExternalOutput").ap()
    xacc_o = nc.dram_tensor("xacc_o", [D, T], F32, kind="ExternalOutput").ap()
    qkm_o = nc.dram_tensor("qkm_o", [1, 2], F32, kind="ExternalOutput").ap()

    with ExitStack() as es:
        sbm = lambda name, shape, dt=F32: es.enter_context(nc.sbuf_tensor(name, shape, dt))[:]
        psm = lambda name, shape, dt=F32: es.enter_context(nc.psum_tensor(name, shape, dt))[:]

        xacc = sbm("xacc", [128, KC * T])
        xn = sbm("xn", [128, KC * TS])
        sq = sbm("sq", [128, TS])
        xtmp = sbm("xtmp", [128, TS])
        mb = sbm("mb", [128, TS])
        rb = sbm("rb", [128, TS])
        stat = sbm("stat", [1, 8 * TS])
        ones128 = sbm("ones128", [128, 1])
        ones1 = sbm("ones1", [1, 128])
        eye = sbm("eye_s", [128, 128])
        cosx = sbm("cosx_s", [ROT, T])
        sinx = sbm("sinx_s", [ROT, T])
        p32 = sbm("p32_s", [ROT, ROT])
        wq = [sbm("wq0", [128, D]), sbm("wq1", [128, D])]
        bias = sbm("bias", [128, 2])
        gt = [sbm("gt0", [128, TS]), sbm("gt1", [128, TS])]
        r1 = sbm("r1", [ROT, TS])
        r2 = sbm("r2", [ROT, TS])
        nm1 = sbm("nm1", [1, 1])
        runm = sbm("runm", [1, 2])
        vt_t = sbm("vt_t", [128, 128])
        wg = sbm("wg", [128, D])
        wu = sbm("wu", [128, D])
        wd = sbm("wd", [128, IGRP * 128])
        gsb = sbm("gsb", [128, 2])
        dsb = sbm("dsb", [128, 16])
        tg = sbm("tg", [128, TS])
        tu = sbm("tu", [128, TS])
        hsb = sbm("hsb", [128, IGRP * TS])
        cpool = sbm("cpool", [128, 4])

        psA = psm("psA", [128, 512])
        psB = psm("psB", [128, 512])
        psR1 = psm("psR1", [1, 512])
        psR2 = psm("psR2", [1, 512])
        psBC = psm("psBC", [128, 512])
        psT = psm("psT", [128, 128])
        psP = psm("psP", [ROT, 512])

        with nc.Block() as blk:
            c = Ctx(nc, es, blk)

            def ldx(sy):
                out = []
                for kc in range(KC):
                    out.append(sy.dma_start(out=xacc[:, kc * T:(kc + 1) * T],
                                            in_=x0_d[kc * 128:(kc + 1) * 128, :]))
                out.append(sy.dma_start(out=cosx, in_=cos_d))
                out.append(sy.dma_start(out=sinx, in_=sin_d))
                out.append(sy.dma_start(out=p32, in_=p32_d))
                out.append(sy.dma_start(out=eye, in_=eye_d))
                for oc in range(16):
                    out.append(sy.dma_start(
                        out=dsb[:, oc:oc + 1],
                        in_=ds_d[oc * 128:(oc + 1) * 128].rearrange("(p one) -> p one", one=1)))
                return out
            c.sy_dma(ldx)

            c.ve(lambda ve: ve.memset(ones128, 1.0))
            c.ve(lambda ve: ve.memset(ones1, 1.0))
            c.ve(lambda ve: ve.memset(runm, 0.0))
            c.ve(lambda ve: ve.memset(cpool[:, 0:1], 1e-5))

            for s in range(NSTRIP):
                def xs(kc, s=s):
                    return xacc[:, kc * T + s * TS: kc * T + s * TS + TS]

                for kc in range(KC):
                    c.se(lambda se, kc=kc: se.activation(sq, xs(kc), AF.Square),
                         ld=c.cnt["ld"])
                    c.te(lambda te, kc=kc: te.matmul(psR1[0:1, :], ones128, xs(kc),
                                                     start=(kc == 0), stop=(kc == KC - 1)))
                    c.te(lambda te, kc=kc: te.matmul(psR2[0:1, :], ones128, sq,
                                                     start=(kc == 0), stop=(kc == KC - 1)))
                m = stat[0:1, 0:TS]
                e2 = stat[0:1, TS:2 * TS]
                var = stat[0:1, 2 * TS:3 * TS]
                sd = stat[0:1, 3 * TS:4 * TS]
                rstd = stat[0:1, 4 * TS:5 * TS]
                c.ve(lambda ve: ve.tensor_scalar(m, psR1[0:1, :], 1.0 / D, None, ALU.mult))
                c.ve(lambda ve: ve.tensor_scalar(e2, psR2[0:1, :], 1.0 / D, None, ALU.mult))
                c.ve(lambda ve: ve.tensor_tensor(var, m, m, ALU.mult))
                c.ve(lambda ve: ve.tensor_sub(var, e2, var))
                c.se(lambda se: se.activation(sd, var, AF.Sqrt, bias=cpool[0:1, 0:1]))
                c.ve(lambda ve: ve.reciprocal(rstd, sd))
                c.te(lambda te: te.matmul(psBC, ones1[0:1, :], m, start=True, stop=True))
                c.se(lambda se: se.copy(mb, psBC))
                c.te(lambda te: te.matmul(psBC, ones1[0:1, :], rstd, start=True, stop=True))
                c.se(lambda se: se.copy(rb, psBC))
                for kc in range(KC):
                    c.ve(lambda ve, kc=kc: ve.tensor_sub(xtmp, xs(kc), mb))
                    c.ve(lambda ve, kc=kc: ve.tensor_mul(xn[:, kc * TS:(kc + 1) * TS], xtmp, rb))

                for h in range(NH0):
                    for r in range(3):
                        gcol = 384 * h + 128 * r
                        idx = h * 3 + r
                        wbuf = wq[idx % 2]
                        gbuf = gt[idx % 2]
                        bcol = bias[:, idx % 2:idx % 2 + 1]

                        c.sy_dma(lambda sy, gcol=gcol, wbuf=wbuf, bcol=bcol: [
                            sy.dma_start(out=wbuf.rearrange("p (kc g) -> p kc g", g=128),
                                         in_=wqkv_d[:, gcol:gcol + 128].rearrange(
                                             "(kc p) g -> p kc g", p=128)),
                            sy.dma_start(out=bcol,
                                         in_=bqkv_d[gcol:gcol + 128].rearrange(
                                             "(p one) -> p one", one=1)),
                        ], te=c.cnt["te"], se=c.cnt["se"])

                        def qkv_mm(te, wbuf=wbuf):
                            for kc in range(KC):
                                ins = te.matmul(psA, wbuf[:, kc * 128:(kc + 1) * 128],
                                                xn[:, kc * TS:(kc + 1) * TS],
                                                start=(kc == 0), stop=(kc == KC - 1))
                            return ins
                        c.te(qkv_mm, ld=c.cnt["ld"])

                        c.se(lambda se, gbuf=gbuf, bcol=bcol:
                             se.activation(gbuf, psA, AF.Identity, bias=bcol),
                             st=c.cnt["st"])

                        if r < 2:
                            sl = slice(s * TS, (s + 1) * TS)
                            c.te(lambda te, gbuf=gbuf: te.matmul(
                                psP, p32, gbuf[0:ROT, :], start=True, stop=True))
                            c.ve(lambda ve, sl=sl: ve.tensor_tensor(
                                r1, psP[:, :], sinx[:, sl], ALU.mult))
                            c.ve(lambda ve, gbuf=gbuf, sl=sl: ve.tensor_tensor(
                                r2, gbuf[0:ROT, :], cosx[:, sl], ALU.mult))
                            c.ve(lambda ve, gbuf=gbuf: ve.tensor_add(gbuf[0:ROT, :], r1, r2))
                            c.se(lambda se, gbuf=gbuf: se.activation(sq, gbuf, AF.Square))
                            c.te(lambda te: te.matmul(psR1[0:1, :], ones128, sq,
                                                      start=True, stop=True))
                            c.ve(lambda ve: ve.reduce_max(nm1, psR1[0:1, :], axis=AX.X))
                            c.ve(lambda ve, r=r: ve.tensor_max(
                                runm[0:1, r:r + 1], runm[0:1, r:r + 1], nm1))
                            c.ge_dma(lambda ge, h=h, r=r, gbuf=gbuf, s=s: ge.dma_start(
                                out=(q_o if r == 0 else k_o)[h * 128:(h + 1) * 128,
                                                             s * TS:(s + 1) * TS],
                                in_=gbuf))
                        else:
                            for j in range(4):
                                c.te(lambda te, gbuf=gbuf, j=j: te.transpose(
                                    psT, gbuf[:, j * 128:(j + 1) * 128], eye))
                                c.se(lambda se: se.copy(vt_t, psT), st=c.cnt["st"])
                                c.ge_dma(lambda ge, h=h, j=j, s=s: ge.dma_start(
                                    out=v_o[s * TS + j * 128: s * TS + (j + 1) * 128,
                                            h * 128:(h + 1) * 128],
                                    in_=vt_t))

                for ig in range(NIG):
                    for ic in range(IGRP):
                        i0 = (ig * IGRP + ic) * 128
                        c.sy_dma(lambda sy, i0=i0: [
                            sy.dma_start(out=wg.rearrange("p (kc g) -> p kc g", g=128),
                                         in_=gwT_d[:, i0:i0 + 128].rearrange(
                                             "(kc p) g -> p kc g", p=128)),
                            sy.dma_start(out=wu.rearrange("p (kc g) -> p kc g", g=128),
                                         in_=uwT_d[:, i0:i0 + 128].rearrange(
                                             "(kc p) g -> p kc g", p=128)),
                            sy.dma_start(out=gsb[:, 0:1], in_=gs_d[i0:i0 + 128].rearrange(
                                "(p one) -> p one", one=1)),
                            sy.dma_start(out=gsb[:, 1:2], in_=us_d[i0:i0 + 128].rearrange(
                                "(p one) -> p one", one=1)),
                        ], te=c.cnt["te"], se=c.cnt["se"])

                        def gu_mm(te):
                            for kc in range(KC):
                                te.matmul(psA, wg[:, kc * 128:(kc + 1) * 128],
                                          xn[:, kc * TS:(kc + 1) * TS],
                                          start=(kc == 0), stop=(kc == KC - 1))
                            for kc in range(KC):
                                ins = te.matmul(psB, wu[:, kc * 128:(kc + 1) * 128],
                                                xn[:, kc * TS:(kc + 1) * TS],
                                                start=(kc == 0), stop=(kc == KC - 1))
                            return ins
                        c.te(gu_mm, ld=c.cnt["ld"])

                        c.se(lambda se: se.activation(tg, psA, AF.Silu, scale=gsb[:, 0:1]))
                        c.se(lambda se: se.activation(tu, psB, AF.Identity, scale=gsb[:, 1:2]))
                        c.ve(lambda ve, ic=ic: ve.tensor_tensor(
                            hsb[:, ic * TS:(ic + 1) * TS], tg, tu, ALU.mult))

                    for oc in range(16):
                        c.sy_dma(lambda sy, ig=ig, oc=oc: [
                            sy.dma_start(out=wd.rearrange("p (ic o) -> p ic o", o=128),
                                         in_=dwT_d[
                                ig * IGRP * 128:(ig + 1) * IGRP * 128,
                                oc * 128:(oc + 1) * 128].rearrange(
                                "(ic p) o -> p ic o", p=128)),
                        ], te=c.cnt["te"])

                        def dn_mm(te):
                            for ic in range(IGRP):
                                ins = te.matmul(psA, wd[:, ic * 128:(ic + 1) * 128],
                                                hsb[:, ic * TS:(ic + 1) * TS],
                                                start=(ic == 0), stop=(ic == IGRP - 1))
                            return ins
                        c.te(dn_mm, ld=c.cnt["ld"])

                        def dn_acc(ve, oc=oc, s=s):
                            xsl = xacc[:, oc * T + s * TS: oc * T + s * TS + TS]
                            return ve.scalar_tensor_tensor(
                                xsl, psA, dsb[:, oc:oc + 1], xsl, ALU.mult, ALU.add)
                        c.ve(dn_acc)

            @blk.gpsimd
            def _(ge):
                for k in ("te", "ve", "se"):
                    if c.cnt[k] > 0:
                        ge.wait_ge(c.sem[k], c.cnt[k])
                for kc in range(KC):
                    c.i("st", ge.dma_start(out=xacc_o[kc * 128:(kc + 1) * 128, :],
                                           in_=xacc[:, kc * T:(kc + 1) * T]), 16)
                c.i("st", ge.dma_start(out=qkm_o, in_=runm), 16)
                ge.wait_ge(c.sem["st"], c.cnt["st"])
    return nc


def _rope_tables(t0, n):
    inv = 1.0 / (10000.0 ** (np.arange(0, ROT, 2, dtype=np.float64) / ROT))
    pos = np.arange(t0, t0 + n, dtype=np.float64)
    fr = np.outer(pos, inv)
    emb = np.concatenate([fr, fr], axis=-1)
    return (np.cos(emb).T.astype(np.float32).copy(),
            np.sin(emb).T.astype(np.float32).copy())


def _p32_lhsT():
    P = np.zeros((ROT, ROT), np.float32)
    half = ROT // 2
    for j in range(half):
        P[j, j + half] = -1.0
        P[j + half, j] = 1.0
    return np.ascontiguousarray(P.T)


def run_l0(inputs):
    x0 = np.asarray(inputs["L0_attn_hidden_states"])
    wqkv = np.ascontiguousarray(np.asarray(inputs["L0_attn_W_qkv"], np.float32))
    bqkv = np.ascontiguousarray(np.asarray(inputs["L0_attn_b_qkv"], np.float32))
    gwT = np.ascontiguousarray(np.asarray(inputs["L0_ffn_gate_w"], np.float32).T)
    uwT = np.ascontiguousarray(np.asarray(inputs["L0_ffn_up_w"], np.float32).T)
    dwT = np.ascontiguousarray(np.asarray(inputs["L0_ffn_down_w"], np.float32).T)
    gs = np.ascontiguousarray(np.asarray(inputs["L0_ffn_gate_s"], np.float32))
    us = np.ascontiguousarray(np.asarray(inputs["L0_ffn_up_s"], np.float32))
    ds = np.ascontiguousarray(np.asarray(inputs["L0_ffn_down_s"], np.float32))
    eye = np.eye(128, dtype=np.float32)
    p32 = _p32_lhsT()
    nc = build_l0()
    in_maps = []
    for core in range(NCORES):
        b, hf = core // 2, core % 2
        t0 = hf * T
        cosx, sinx = _rope_tables(t0, T)
        x0T = np.ascontiguousarray(x0[b, t0:t0 + T, :].T)
        in_maps.append(dict(x0=x0T, wqkv=wqkv, bqkv=bqkv, gwT=gwT, uwT=uwT,
                            dwT=dwT, gs=gs, us=us, ds=ds, cosx=cosx, sinx=sinx,
                            p32=p32, eye=eye))
    res = run_bass_kernel_spmd(nc, in_maps, core_ids=list(range(NCORES)))
    return res.results




def build_l2():
    nc = bass.Bass("TRN2", target_bir_lowering=False, debug=False)
    q_d = nc.dram_tensor("q", [D, T], F32, kind="ExternalInput").ap()
    kf_d = nc.dram_tensor("kf", [D, 2 * T], F32, kind="ExternalInput").ap()
    vf_d = nc.dram_tensor("vf", [2 * T, D], F32, kind="ExternalInput").ap()
    mask_d = nc.dram_tensor("mask", [2 * T, T], F32, kind="ExternalInput").ap()
    xacc_d = nc.dram_tensor("xacc", [D, T], F32, kind="ExternalInput").ap()
    wden_d = nc.dram_tensor("wden", [D, D], F32, kind="ExternalInput").ap()
    bden_d = nc.dram_tensor("bden", [D], F32, kind="ExternalInput").ap()
    qkm_d = nc.dram_tensor("qkm3", [1, 3], F32, kind="ExternalInput").ap()
    x1_o = nc.dram_tensor("x1_o", [D, T], F32, kind="ExternalOutput").ap()

    NKT = (2 * T) // 128

    with ExitStack() as es:
        sbm = lambda name, shape, dt=F32: es.enter_context(nc.sbuf_tensor(name, shape, dt))[:]
        psm = lambda name, shape, dt=F32: es.enter_context(nc.psum_tensor(name, shape, dt))[:]

        xacc = sbm("xac_s", [128, KC * T])
        qh_s = sbm("qh_s", [128, KC * TS])
        mask_s = sbm("mask_s", [128, NKT * TS])
        kbuf = sbm("kbuf", [128, 2 * T])
        vbuf = sbm("vbuf", [128, NKT * 128])
        expb = [sbm("expb0", [128, TS]), sbm("expb1", [128, TS])]
        attn_s = sbm("attn_s", [128, NH0 * TS])
        rbc = sbm("rbc", [128, TS])
        wdn = sbm("wdn", [128, D])
        bdn = sbm("bdn", [128, 16])
        qkm = sbm("qkm", [1, 3])
        t11 = sbm("t11", [1, 2])
        c0b = sbm("c0b", [128, 1])
        ones128 = sbm("ones128", [128, 1])
        ones1 = sbm("ones1", [1, 128])

        psS = psm("psS", [128, 512])
        psPV = psm("psPV", [128, 512])
        psR = psm("psR", [1, 512])
        psBC = psm("psBC", [128, 512])
        psD = psm("psD", [128, 512])

        with nc.Block() as blk:
            c = Ctx(nc, es, blk)

            def ld0(sy):
                out = []
                for kc in range(KC):
                    out.append(sy.dma_start(out=xacc[:, kc * T:(kc + 1) * T],
                                            in_=xacc_d[kc * 128:(kc + 1) * 128, :]))
                out.append(sy.dma_start(out=qkm, in_=qkm_d))
                for oc in range(16):
                    out.append(sy.dma_start(
                        out=bdn[:, oc:oc + 1],
                        in_=bden_d[oc * 128:(oc + 1) * 128].rearrange("(p one) -> p one", one=1)))
                return out
            c.sy_dma(ld0)
            c.ve(lambda ve: ve.memset(ones128, 1.0))
            c.ve(lambda ve: ve.memset(ones1, 1.0))

            c.ve(lambda ve: ve.tensor_max(t11[0:1, 0:1], qkm[0:1, 1:2], qkm[0:1, 2:3]),
                 ld=c.cnt["ld"])
            c.ve(lambda ve: ve.tensor_mul(t11[0:1, 0:1], t11[0:1, 0:1], qkm[0:1, 0:1]))
            c.se(lambda se: se.activation(t11[0:1, 1:2], t11[0:1, 0:1], AF.Sqrt))
            c.ve(lambda ve: ve.tensor_scalar(t11[0:1, 0:1], t11[0:1, 1:2],
                                             -1.0 / math.sqrt(HD), None, ALU.mult))
            c.te(lambda te: te.matmul(psBC[:, 0:1], ones1[0:1, :], t11[0:1, 0:1],
                                      start=True, stop=True))
            c.se(lambda se: se.copy(c0b, psBC[:, 0:1]))

            for qh in range(NSTRIP):
                def ldq(sy, qh=qh):
                    out = []
                    for kc in range(KC):
                        out.append(sy.dma_start(
                            out=qh_s[:, kc * TS:(kc + 1) * TS],
                            in_=q_d[kc * 128:(kc + 1) * 128, qh * TS:(qh + 1) * TS]))
                    for kt in range(NKT):
                        out.append(sy.dma_start(
                            out=mask_s[:, kt * TS:(kt + 1) * TS],
                            in_=mask_d[kt * 128:(kt + 1) * 128, qh * TS:(qh + 1) * TS]))
                    return out
                c.sy_dma(ldq, te=c.cnt["te"], se=c.cnt["se"], ve=c.cnt["ve"])

                for h in range(NH0):
                    def ldkv(sy, h=h):
                        return [
                            sy.dma_start(out=kbuf, in_=kf_d[h * 128:(h + 1) * 128, :]),
                            sy.dma_start(out=vbuf.rearrange("p (kt d) -> p kt d", d=128),
                                         in_=vf_d[:, h * 128:(h + 1) * 128].rearrange(
                                             "(kt p) d -> p kt d", p=128)),
                        ]
                    c.sy_dma(ldkv, te=c.cnt["te"], se=c.cnt["se"])

                    for kt in range(NKT):
                        eb = expb[kt % 2]
                        c.te(lambda te, h=h, kt=kt: te.matmul(
                            psS, kbuf[:, kt * 128:(kt + 1) * 128],
                            qh_s[:, h * TS:(h + 1) * TS],
                            start=True, stop=True), ld=c.cnt["ld"])
                        c.ve(lambda ve, kt=kt: ve.tensor_add(
                            psS, psS, mask_s[:, kt * TS:(kt + 1) * TS]))
                        c.se(lambda se, eb=eb: se.activation(
                            eb, psS, AF.Exp, bias=c0b, scale=1.0 / math.sqrt(HD)))
                        c.te(lambda te, kt=kt, eb=eb: te.matmul(
                            psR[0:1, :], ones128, eb,
                            start=(kt == 0), stop=(kt == NKT - 1)))
                        c.te(lambda te, kt=kt, eb=eb: te.matmul(
                            psPV, vbuf[:, kt * 128:(kt + 1) * 128], eb,
                            start=(kt == 0), stop=(kt == NKT - 1)))

                    c.ve(lambda ve: ve.reciprocal(rbc[0:1, :], psR[0:1, :]))
                    c.te(lambda te: te.matmul(psBC, ones1[0:1, :], rbc[0:1, :],
                                              start=True, stop=True))
                    c.se(lambda se: se.copy(rbc, psBC))
                    c.ve(lambda ve, h=h: ve.tensor_mul(
                        attn_s[:, h * TS:(h + 1) * TS], psPV, rbc))

                for oc in range(16):
                    c.sy_dma(lambda sy, oc=oc: [
                        sy.dma_start(out=wdn.rearrange("p (kc g) -> p kc g", g=128),
                                     in_=wden_d[:, oc * 128:(oc + 1) * 128].rearrange(
                                         "(kc p) g -> p kc g", p=128)),
                    ], te=c.cnt["te"])

                    def dmm(te):
                        for hc in range(KC):
                            ins = te.matmul(psD, wdn[:, hc * 128:(hc + 1) * 128],
                                            attn_s[:, hc * TS:(hc + 1) * TS],
                                            start=(hc == 0), stop=(hc == KC - 1))
                        return ins
                    c.te(dmm, ld=c.cnt["ld"])

                    def dacc(ve, oc=oc, qh=qh):
                        xsl = xacc[:, oc * T + qh * TS: oc * T + qh * TS + TS]
                        return ve.scalar_tensor_tensor(
                            xsl, psD, bdn[:, oc:oc + 1], xsl, ALU.add, ALU.add)
                    c.ve(dacc)

            @blk.gpsimd
            def _(ge):
                for k in ("te", "ve", "se"):
                    if c.cnt[k] > 0:
                        ge.wait_ge(c.sem[k], c.cnt[k])
                for kc in range(KC):
                    c.i("st", ge.dma_start(out=x1_o[kc * 128:(kc + 1) * 128, :],
                                           in_=xacc[:, kc * T:(kc + 1) * T]), 16)
                ge.wait_ge(c.sem["st"], c.cnt["st"])
    return nc


def run_l2(inputs, l0res):
    wden = np.ascontiguousarray(np.asarray(inputs["L0_attn_W_dense"], np.float32))
    bden = np.ascontiguousarray(np.asarray(inputs["L0_attn_b_dense"], np.float32))
    nc = build_l2()
    in_maps = []
    for core in range(NCORES):
        b, hf = core // 2, core % 2
        t0 = hf * T
        ra, rb_ = l0res[2 * b], l0res[2 * b + 1]
        r = l0res[core]
        kf = np.concatenate([ra["k_o"], rb_["k_o"]], axis=1)
        vf = np.concatenate([ra["v_o"], rb_["v_o"]], axis=0)
        kk = np.arange(2 * T)[:, None]
        qq = np.arange(T)[None, :]
        mask = np.where(kk <= t0 + qq, 0.0, -1e30).astype(np.float32)
        qkm3 = np.array([[r["qkm_o"][0, 0], ra["qkm_o"][0, 1], rb_["qkm_o"][0, 1]]],
                        np.float32)
        in_maps.append(dict(q=r["q_o"], kf=kf, vf=vf, mask=mask, xacc=r["xacc_o"],
                            wden=wden, bden=bden, qkm3=qkm3))
    res = run_bass_kernel_spmd(nc, in_maps, core_ids=list(range(NCORES)))
    return res.results




NZC = 32
NXBC = 34
PROJC = NZC + NXBC


def build_l3a():
    nc = bass.Bass("TRN2", target_bir_lowering=False, debug=False)
    x1_d = nc.dram_tensor("x1", [D, T], F32, kind="ExternalInput").ap()
    winT_d = nc.dram_tensor("winT", [D, 8512], F32, kind="ExternalInput").ap()
    wT_d = nc.dram_tensor("wT", [D, D], F32, kind="ExternalInput").ap()
    aT_d = nc.dram_tensor("aT", [D, 256], F32, kind="ExternalInput").ap()
    rw1T_d = nc.dram_tensor("rw1T", [D, 64], F32, kind="ExternalInput").ap()
    rw2T_d = nc.dram_tensor("rw2T", [64, 32], F32, kind="ExternalInput").ap()
    rb1_d = nc.dram_tensor("rb1", [64], F32, kind="ExternalInput").ap()
    rb2_d = nc.dram_tensor("rb2", [16], F32, kind="ExternalInput").ap()
    gates_d = nc.dram_tensor("gates", [16], F32, kind="ExternalInput").ap()
    bias_d = nc.dram_tensor("biasW", [D], F32, kind="ExternalInput").ap()
    bflat_d = nc.dram_tensor("bflat", [256, D], F32, kind="ExternalInput").ap()
    e16_d = nc.dram_tensor("e16", [16, 256], F32, kind="ExternalInput").ap()
    eye_d = nc.dram_tensor("eye", [128, 128], F32, kind="ExternalInput").ap()

    xacc_o = nc.dram_tensor("xacc_o", [D, T], F32, kind="ExternalOutput").ap()
    z_o = nc.dram_tensor("z_o", [4096, T], F32, kind="ExternalOutput").ap()
    xbc_o = nc.dram_tensor("xbc_o", [4352, T], F32, kind="ExternalOutput").ap()
    dt_o = nc.dram_tensor("dt_o", [64, T], F32, kind="ExternalOutput").ap()

    with ExitStack() as es:
        sbm = lambda name, shape, dt=F32: es.enter_context(nc.sbuf_tensor(name, shape, dt))[:]
        psm = lambda name, shape, dt=F32: es.enter_context(nc.psum_tensor(name, shape, dt))[:]

        xacc = sbm("xac_s", [128, KC * T])
        xn = sbm("xn", [128, KC * TS])
        sq = sbm("sq", [128, TS])
        xtmp = sbm("xtmp", [128, TS])
        mb = sbm("mb", [128, TS])
        rbb = sbm("rbb", [128, TS])
        stat = sbm("stat", [1, 8 * TS])
        ones128 = sbm("ones128", [128, 1])
        ones1 = sbm("ones1", [1, 128])
        eye = sbm("eye_s", [128, 128])
        cpool = sbm("cpool", [128, 4])
        wbuf = [sbm("wb0", [128, D]), sbm("wb1", [128, D])]
        otile = sbm("otile", [128, TS])
        xa = sbm("xa", [128, 2 * TS])
        hr = sbm("hr", [64, TS])
        lgt = sbm("lgt", [128, 4 * 16])
        coeff = sbm("coeff", [16, TS])
        co256 = sbm("co256", [128, 2 * TS])
        e16 = sbm("e16_s", [16, 256])
        rbias = sbm("rbias", [128, 4])
        bw = sbm("bw", [128, 16])
        wbf = sbm("wbf", [128, 2 * 128])
        t1 = sbm("t1", [128, 16])
        t2 = sbm("t2", [128, 16])
        t3 = sbm("t3", [128, 8])

        psA = psm("psA", [128, 512])
        psD = psm("psD", [128, 512])
        psR1 = psm("psR1", [1, 512])
        psR2 = psm("psR2", [1, 512])
        psBC = psm("psBC", [128, 512])
        psT = psm("psT", [128, 128])

        with nc.Block() as blk:
            c = Ctx(nc, es, blk)

            def ld0(sy):
                out = []
                for kc in range(KC):
                    out.append(sy.dma_start(out=xacc[:, kc * T:(kc + 1) * T],
                                            in_=x1_d[kc * 128:(kc + 1) * 128, :]))
                out.append(sy.dma_start(out=eye, in_=eye_d))
                out.append(sy.dma_start(out=e16, in_=e16_d))
                out.append(sy.dma_start(out=rbias[0:64, 0:1],
                                        in_=rb1_d.rearrange("(p one) -> p one", one=1)))
                out.append(sy.dma_start(out=rbias[0:16, 1:2],
                                        in_=rb2_d.rearrange("(p one) -> p one", one=1)))
                out.append(sy.dma_start(out=rbias[0:16, 2:3],
                                        in_=gates_d.rearrange("(p one) -> p one", one=1)))
                for ocb in range(16):
                    out.append(sy.dma_start(
                        out=bw[:, ocb:ocb + 1],
                        in_=bias_d[ocb * 128:(ocb + 1) * 128].rearrange("(p one) -> p one", one=1)))
                return out
            c.sy_dma(ld0)
            c.ve(lambda ve: ve.memset(ones128, 1.0))
            c.ve(lambda ve: ve.memset(ones1, 1.0))
            c.ve(lambda ve: ve.memset(cpool[:, 0:1], 1e-5))
            c.ve(lambda ve: ve.tensor_add(rbias[0:16, 1:2], rbias[0:16, 1:2],
                                          rbias[0:16, 2:3]), ld=c.cnt["ld"])
            c.ve(lambda ve: ve.tensor_add(rbias[0:16, 1:2], rbias[0:16, 1:2],
                                          rbias[0:16, 2:3]), ld=c.cnt["ld"])

            for s in range(NSTRIP):
                def xsl_(kc, s=s):
                    return xacc[:, kc * T + s * TS: kc * T + s * TS + TS]

                for kc in range(KC):
                    c.se(lambda se, kc=kc: se.activation(sq, xsl_(kc), AF.Square),
                         ld=c.cnt["ld"])
                    c.te(lambda te, kc=kc: te.matmul(psR1[0:1, :], ones128, xsl_(kc),
                                                     start=(kc == 0), stop=(kc == KC - 1)))
                    c.te(lambda te, kc=kc: te.matmul(psR2[0:1, :], ones128, sq,
                                                     start=(kc == 0), stop=(kc == KC - 1)))
                m = stat[0:1, 0:TS]; e2 = stat[0:1, TS:2 * TS]
                var = stat[0:1, 2 * TS:3 * TS]; sd = stat[0:1, 3 * TS:4 * TS]
                rstd = stat[0:1, 4 * TS:5 * TS]
                c.ve(lambda ve: ve.tensor_scalar(m, psR1[0:1, :], 1.0 / D, None, ALU.mult))
                c.ve(lambda ve: ve.tensor_scalar(e2, psR2[0:1, :], 1.0 / D, None, ALU.mult))
                c.ve(lambda ve: ve.tensor_tensor(var, m, m, ALU.mult))
                c.ve(lambda ve: ve.tensor_sub(var, e2, var))
                c.se(lambda se: se.activation(sd, var, AF.Sqrt, bias=cpool[0:1, 0:1]))
                c.ve(lambda ve: ve.reciprocal(rstd, sd))
                c.te(lambda te: te.matmul(psBC, ones1[0:1, :], m, start=True, stop=True))
                c.se(lambda se: se.copy(mb, psBC))
                c.te(lambda te: te.matmul(psBC, ones1[0:1, :], rstd, start=True, stop=True))
                c.se(lambda se: se.copy(rbb, psBC))
                for kc in range(KC):
                    c.ve(lambda ve, kc=kc: ve.tensor_sub(xtmp, xsl_(kc), mb))
                    c.ve(lambda ve, kc=kc: ve.tensor_mul(xn[:, kc * TS:(kc + 1) * TS], xtmp, rbb))

                def wload(sy, dram, col0, width, buf):
                    return [sy.dma_start(
                        out=buf[:, 0:KC * width].rearrange("p (kc g) -> p kc g", g=width),
                        in_=dram[:, col0:col0 + width].rearrange("(kc p) g -> p kc g", p=128))]

                def projmm(te, buf, width):
                    for kc in range(KC):
                        ins = te.matmul(psA[0:width if width < 128 else 128, :],
                                        buf[:, kc * width:(kc + 1) * width],
                                        xn[:, kc * TS:(kc + 1) * TS],
                                        start=(kc == 0), stop=(kc == KC - 1))
                    return ins

                for pc in range(PROJC + 1):
                    width = 64 if pc == PROJC else 128
                    col0 = pc * 128
                    buf = wbuf[pc % 2]
                    c.sy_dma(lambda sy, col0=col0, width=width, buf=buf:
                             wload(sy, winT_d, col0, width, buf),
                             te=c.cnt["te"])
                    c.te(lambda te, buf=buf, width=width: projmm(te, buf, width),
                         ld=c.cnt["ld"])
                    c.se(lambda se, width=width: se.copy(otile[0:width, :], psA[0:width, :]),
                         st=c.cnt["st"])
                    if pc < NZC:
                        dst, row = z_o, pc * 128
                    elif pc < PROJC:
                        dst, row = xbc_o, (pc - NZC) * 128
                    else:
                        dst, row = dt_o, 0
                    c.ge_dma(lambda ge, dst=dst, row=row, width=width, s=s: ge.dma_start(
                        out=dst[row:row + width, s * TS:(s + 1) * TS],
                        in_=otile[0:width, :]))

                for ocb in range(16):
                    buf = wbuf[ocb % 2]
                    c.sy_dma(lambda sy, ocb=ocb, buf=buf:
                             wload(sy, wT_d, ocb * 128, 128, buf), te=c.cnt["te"])
                    c.te(lambda te, buf=buf: projmm(te, buf, 128), ld=c.cnt["ld"])
                    def bacc(ve, ocb=ocb, s=s):
                        xsl = xacc[:, ocb * T + s * TS: ocb * T + s * TS + TS]
                        return ve.scalar_tensor_tensor(
                            xsl, psA, bw[:, ocb:ocb + 1], xsl, ALU.add, ALU.add)
                    c.ve(bacc)

                for ac in range(2):
                    buf = wbuf[ac % 2]
                    c.sy_dma(lambda sy, ac=ac, buf=buf:
                             wload(sy, aT_d, ac * 128, 128, buf), te=c.cnt["te"])
                    c.te(lambda te, buf=buf: projmm(te, buf, 128), ld=c.cnt["ld"])
                    c.se(lambda se, ac=ac: se.copy(xa[:, ac * TS:(ac + 1) * TS], psA))

                c.sy_dma(lambda sy: wload(sy, rw1T_d, 0, 64, wbuf[0]), te=c.cnt["te"])
                c.te(lambda te: projmm(te, wbuf[0], 64), ld=c.cnt["ld"])
                c.se(lambda se: se.activation(hr, psA[0:64, :], AF.Silu,
                                              bias=rbias[0:64, 0:1]))
                c.sy_dma(lambda sy: [sy.dma_start(out=wbf[0:64, 0:32], in_=rw2T_d)],
                         te=c.cnt["te"])
                c.te(lambda te: te.matmul(psA[0:32, :], wbf[0:64, 0:32], hr,
                                          start=True, stop=True), ld=c.cnt["ld"])
                c.se(lambda se: se.activation(otile[0:16, :], psA[0:16, :], AF.Identity,
                                              bias=rbias[0:16, 1:2]))
                for tt in range(4):
                    c.te(lambda te, tt=tt: te.transpose(
                        psT[:, 0:16], otile[0:16, tt * 128:(tt + 1) * 128],
                        eye[0:16, 0:16]))
                    c.se(lambda se, tt=tt: se.copy(lgt[:, tt * 16:(tt + 1) * 16],
                                                   psT[:, 0:16]))
                for tt in range(4):
                    lt = lgt[:, tt * 16:(tt + 1) * 16]
                    m1 = t3[:, 0:1]; m2 = t3[:, 1:2]; dm = t3[:, 2:3]
                    e2c = t3[:, 3:4]; den = t3[:, 4:5]; w1 = t3[:, 5:6]; w2 = t3[:, 6:7]
                    c.ve(lambda ve, lt=lt: ve.reduce_max(m1, lt, axis=AX.X))
                    c.ve(lambda ve, lt=lt: ve.tensor_scalar(t1, lt, m1, None, ALU.is_ge))
                    c.ve(lambda ve, lt=lt: ve.scalar_tensor_tensor(
                        t2, t1, -1e30, lt, ALU.mult, ALU.add))
                    c.ve(lambda ve: ve.reduce_max(m2, t2, axis=AX.X))
                    c.ve(lambda ve: ve.tensor_scalar(t2, t2, m2, None, ALU.is_ge))
                    c.ve(lambda ve: ve.tensor_sub(dm, m2, m1))
                    c.se(lambda se: se.activation(e2c, dm, AF.Exp))
                    c.ve(lambda ve: ve.tensor_scalar(den, e2c, 1.0, None, ALU.add))
                    c.ve(lambda ve: ve.reciprocal(w1, den))
                    c.ve(lambda ve: ve.tensor_mul(w2, e2c, w1))
                    c.ve(lambda ve: ve.tensor_scalar(t1, t1, w1, None, ALU.mult))
                    c.ve(lambda ve: ve.tensor_scalar(t2, t2, w2, None, ALU.mult))
                    c.ve(lambda ve: ve.tensor_add(t1, t1, t2))
                    c.te(lambda te: te.transpose(psT[0:16, 0:128], t1, eye))
                    c.se(lambda se, tt=tt: se.copy(coeff[0:16, tt * 128:(tt + 1) * 128],
                                                   psT[0:16, 0:128]))
                for ec in range(2):
                    c.te(lambda te, ec=ec: te.matmul(
                        psA, e16[:, ec * 128:(ec + 1) * 128], coeff[0:16, :],
                        start=True, stop=True))
                    c.se(lambda se, ec=ec: se.copy(co256[:, ec * TS:(ec + 1) * TS], psA))
                for ec in range(2):
                    c.ve(lambda ve, ec=ec: ve.tensor_mul(
                        xa[:, ec * TS:(ec + 1) * TS], xa[:, ec * TS:(ec + 1) * TS],
                        co256[:, ec * TS:(ec + 1) * TS]))
                for ocb in range(16):
                    c.sy_dma(lambda sy, ocb=ocb: [sy.dma_start(
                        out=wbf.rearrange("p (ec g) -> p ec g", g=128),
                        in_=bflat_d[:, ocb * 128:(ocb + 1) * 128].rearrange(
                            "(ec p) g -> p ec g", p=128))], te=c.cnt["te"])
                    def dmm(te):
                        for ec in range(2):
                            ins = te.matmul(psD, wbf[:, ec * 128:(ec + 1) * 128],
                                            xa[:, ec * TS:(ec + 1) * TS],
                                            start=(ec == 0), stop=(ec == 1))
                        return ins
                    c.te(dmm, ld=c.cnt["ld"])
                    def dacc(ve, ocb=ocb, s=s):
                        xsl = xacc[:, ocb * T + s * TS: ocb * T + s * TS + TS]
                        return ve.scalar_tensor_tensor(
                            xsl, psD, 1.0, xsl, ALU.mult, ALU.add)
                    c.ve(dacc)

            @blk.gpsimd
            def _(ge):
                for k in ("te", "ve", "se"):
                    if c.cnt[k] > 0:
                        ge.wait_ge(c.sem[k], c.cnt[k])
                for kc in range(KC):
                    c.i("st", ge.dma_start(out=xacc_o[kc * 128:(kc + 1) * 128, :],
                                           in_=xacc[:, kc * T:(kc + 1) * T]), 16)
                ge.wait_ge(c.sem["st"], c.cnt["st"])
    return nc


def run_l3a(inputs, x1_cores):
    winT = np.ascontiguousarray(np.asarray(inputs["L1_attn_W_in"], np.float32).T)
    wT = np.ascontiguousarray(np.asarray(inputs["L1_ffn_W"], np.float32).T)
    aT = np.ascontiguousarray(np.asarray(inputs["L1_ffn_A"], np.float32).reshape(256, D).T)
    rw1T = np.ascontiguousarray(np.asarray(inputs["L1_ffn_rw1"], np.float32).T)
    rw2T = np.ascontiguousarray(np.asarray(inputs["L1_ffn_rw2"], np.float32).T)
    rb1 = np.ascontiguousarray(np.asarray(inputs["L1_ffn_rb1"], np.float32))
    rb2 = np.ascontiguousarray(np.asarray(inputs["L1_ffn_rb2"], np.float32)[:16])
    gates = np.ascontiguousarray(np.asarray(inputs["L1_ffn_gates"], np.float32))
    bias = np.ascontiguousarray(np.asarray(inputs["L1_ffn_bias"], np.float32))
    bflat = np.ascontiguousarray(
        np.asarray(inputs["L1_ffn_B"], np.float32).transpose(0, 2, 1).reshape(256, D))
    e16 = np.zeros((16, 256), np.float32)
    for e in range(16):
        e16[e, e * 16:(e + 1) * 16] = 1.0
    eye = np.eye(128, dtype=np.float32)
    nc = build_l3a()
    in_maps = []
    for core in range(NCORES):
        in_maps.append(dict(x1=x1_cores[core], winT=winT, wT=wT, aT=aT, rw1T=rw1T,
                            rw2T=rw2T, rb1=rb1, rb2=rb2, gates=gates, biasW=bias, bflat=bflat,
                            e16=e16, eye=eye))
    res = run_bass_kernel_spmd(nc, in_maps, core_ids=list(range(NCORES)))
    return res.results




CH = 256
NCH = 4
NHP = 32


def build_l3b():
    nc = bass.Bass("TRN2", target_bir_lowering=False, debug=False)
    xbc_d = nc.dram_tensor("xbc", [4352, T], F32, kind="ExternalInput").ap()
    halo_d = nc.dram_tensor("halo", [4352, 3], F32, kind="ExternalInput").ap()
    cw_d = nc.dram_tensor("cw", [4352, 4], F32, kind="ExternalInput").ap()
    cb_d = nc.dram_tensor("cb", [4352], F32, kind="ExternalInput").ap()
    dtr_d = nc.dram_tensor("dtr", [64, T], F32, kind="ExternalInput").ap()
    dtb_d = nc.dram_tensor("dtb", [64], F32, kind="ExternalInput").ap()
    alog_d = nc.dram_tensor("alog", [64], F32, kind="ExternalInput").ap()
    eye_d = nc.dram_tensor("eye", [128, 128], F32, kind="ExternalInput").ap()

    bfm_o = nc.dram_tensor("bfm_o", [128, T], F32, kind="ExternalOutput").ap()
    cfm_o = nc.dram_tensor("cfm_o", [128, T], F32, kind="ExternalOutput").ap()
    xct_o = nc.dram_tensor("xct_o", [T, 4096], F32, kind="ExternalOutput").ap()
    xst_o = nc.dram_tensor("xst_o", [T, 4096], F32, kind="ExternalOutput").ap()
    xsdt_o = nc.dram_tensor("xsdt_o", [T, 4096], F32, kind="ExternalOutput").ap()
    acs_o = nc.dram_tensor("acs_o", [64, T], F32, kind="ExternalOutput").ap()
    sinl_o = nc.dram_tensor("sinl_o", [NCH * 128, 4096], F32, kind="ExternalOutput").ap()
    sloc_o = nc.dram_tensor("sloc_o", [128, 4096], F32, kind="ExternalOutput").ap()
    pfac_o = nc.dram_tensor("pfac_o", [1, NCH * 64], F32, kind="ExternalOutput").ap()

    with ExitStack() as es:
        sbm = lambda name, shape, dt=F32: es.enter_context(nc.sbuf_tensor(name, shape, dt))[:]
        psm = lambda name, shape, dt=F32: es.enter_context(nc.psum_tensor(name, shape, dt))[:]

        ext = sbm("ext", [128, 3 + T])
        acc = sbm("acc", [128, T])
        xcc = sbm("xcc", [128, T])
        wc4 = sbm("wc4", [128, 4])
        cbc = sbm("cbc", [128, 1])
        eye = sbm("eye_s", [128, 128])
        cpool = sbm("cpool", [128, 4])
        dtb_s = sbm("dtb_s", [64, 1])
        a64 = sbm("a64", [64, 1])
        dtv = sbm("dtv", [64, T])
        tmp64 = sbm("tmp64", [64, T])
        tmp64b = sbm("tmp64b", [64, T])
        acs = sbm("acs", [64, T])
        cum = [sbm("cum0", [64, CH]), sbm("cum1", [64, CH])]
        f3 = sbm("f3", [64, T])
        dtt = sbm("dtt", [128, 8 * 64])
        f3t = sbm("f3t", [128, 8 * 64])
        x3t = sbm("x3t", [128, 128])
        erow = sbm("erow", [1, NCH * 64])
        prow = sbm("prow", [1, (NCH + 1) * 64])
        eab = sbm("eab", [128, 64])
        xtt = sbm("xtt", [128, 128])
        x2t = sbm("x2t", [128, 128])
        btm = sbm("btm", [128, 2 * 128])
        xs2c = sbm("xs2c", [128, 2 * 4096])
        S = sbm("S_s", [128, 4096])
        ones1 = sbm("ones1", [1, 128])

        psT = psm("psT", [128, 128])
        psSt = psm("psSt", [128, 2048])
        psBC = psm("psBC", [128, 64])

        with nc.Block() as blk:
            c = Ctx(nc, es, blk)

            def ld0(sy):
                return [
                    sy.dma_start(out=eye, in_=eye_d),
                    sy.dma_start(out=dtv, in_=dtr_d),
                    sy.dma_start(out=dtb_s, in_=dtb_d.rearrange("(p one) -> p one", one=1)),
                    sy.dma_start(out=a64, in_=alog_d.rearrange("(p one) -> p one", one=1)),
                ]
            c.sy_dma(ld0)
            c.ve(lambda ve: ve.memset(cpool[:, 0:1], 1.0))
            c.ve(lambda ve: ve.memset(ones1, 1.0))
            c.ve(lambda ve: ve.memset(S, 0.0))
            c.ve(lambda ve: ve.memset(prow[0:1, 0:64], 1.0))

            c.ve(lambda ve: ve.tensor_scalar(dtv, dtv, dtb_s, None, ALU.add),
                 ld=c.cnt["ld"])
            c.se(lambda se: se.activation(tmp64, dtv, AF.Abs))
            c.se(lambda se: se.activation(tmp64, tmp64, AF.Exp, scale=-1.0))
            c.se(lambda se: se.activation(tmp64, tmp64, AF.Ln, bias=cpool[0:64, 0:1]))
            c.ve(lambda ve: ve.tensor_relu(tmp64b, dtv))
            c.ve(lambda ve: ve.tensor_add(dtv, tmp64b, tmp64))
            c.ve(lambda ve: ve.tensor_scalar_min(dtv, dtv, 1e10))
            c.se(lambda se: se.activation(a64, a64, AF.Exp))
            c.ve(lambda ve: ve.tensor_scalar(a64, a64, -1.0, None, ALU.mult))
            c.ve(lambda ve: ve.tensor_scalar(tmp64, dtv, a64, None, ALU.mult))
            for ch_ in range(NCH):
                c.ve(lambda ve, ch_=ch_: ve.tensor_copy(
                    cum[0], tmp64[:, ch_ * CH:(ch_ + 1) * CH]))
                cur = 0
                k = 1
                while k < CH:
                    c.ve(lambda ve, cur=cur, k=k: ve.tensor_copy(
                        cum[1 - cur][0:64, 0:k], cum[cur][0:64, 0:k]))
                    c.ve(lambda ve, cur=cur, k=k: ve.tensor_add(
                        cum[1 - cur][0:64, k:CH], cum[cur][0:64, k:CH],
                        cum[cur][0:64, 0:CH - k]))
                    cur = 1 - cur
                    k *= 2
                c.ve(lambda ve, ch_=ch_, cur=cur: ve.tensor_copy(
                    acs[:, ch_ * CH:(ch_ + 1) * CH], cum[cur]))
            c.ge_dma(lambda ge: ge.dma_start(out=acs_o, in_=acs[0:64, :]))
            for lt in range(8):
                c.te(lambda te, lt=lt: te.transpose(
                    psT[:, 0:64], dtv[0:64, lt * 128:(lt + 1) * 128], eye[0:64, 0:64]))
                c.se(lambda se, lt=lt: se.copy(dtt[:, lt * 64:(lt + 1) * 64], psT[:, 0:64]))
            for ch_ in range(NCH):
                sl_ = slice(ch_ * CH, (ch_ + 1) * CH)
                c.ve(lambda ve, ch_=ch_, sl_=sl_: ve.tensor_scalar(
                    f3[:, sl_], acs[0:64, sl_],
                    acs[0:64, ch_ * CH + CH - 1:ch_ * CH + CH], None, ALU.subtract))
            c.se(lambda se: se.activation(f3, f3[0:64, :], AF.Exp, scale=-1.0))
            c.ve(lambda ve: ve.tensor_mul(f3, dtv, f3))
            for lt in range(8):
                c.te(lambda te, lt=lt: te.transpose(
                    psT[:, 0:64], f3[0:64, lt * 128:(lt + 1) * 128], eye[0:64, 0:64]))
                c.se(lambda se, lt=lt: se.copy(f3t[:, lt * 64:(lt + 1) * 64], psT[:, 0:64]))
            for ch_ in range(NCH):
                c.se(lambda se, ch_=ch_: se.activation(
                    a64, acs[0:64, ch_ * CH + CH - 1:ch_ * CH + CH], AF.Exp))
                c.te(lambda te: te.transpose(psT[0:1, 0:64], a64, eye[0:64, 0:64]))
                c.se(lambda se, ch_=ch_: se.copy(
                    erow[0:1, ch_ * 64:(ch_ + 1) * 64], psT[0:1, 0:64]))
                c.ve(lambda ve, ch_=ch_: ve.tensor_mul(
                    prow[0:1, (ch_ + 1) * 64:(ch_ + 2) * 64],
                    prow[0:1, ch_ * 64:(ch_ + 1) * 64],
                    erow[0:1, ch_ * 64:(ch_ + 1) * 64]))
            c.ge_dma(lambda ge: ge.dma_start(out=pfac_o, in_=prow[0:1, 0:NCH * 64]))

            for ch_ in range(NCH + 30):
                def ldc(sy, ch_=ch_):
                    return [
                        sy.dma_start(out=ext[:, 3:3 + T],
                                     in_=xbc_d[ch_ * 128:(ch_ + 1) * 128, :]),
                        sy.dma_start(out=ext[:, 0:3],
                                     in_=halo_d[ch_ * 128:(ch_ + 1) * 128, :]),
                        sy.dma_start(out=wc4, in_=cw_d[ch_ * 128:(ch_ + 1) * 128, :]),
                        sy.dma_start(out=cbc, in_=cb_d[ch_ * 128:(ch_ + 1) * 128]
                                     .rearrange("(p one) -> p one", one=1)),
                    ]
                c.sy_dma(ldc, ve=c.cnt["ve"], se=c.cnt["se"], te=c.cnt["te"])
                c.ve(lambda ve: ve.tensor_scalar(acc, ext[:, 0:T], wc4[:, 0:1],
                                                 None, ALU.mult), ld=c.cnt["ld"])
                for j in range(1, 4):
                    c.ve(lambda ve, j=j: ve.scalar_tensor_tensor(
                        acc, ext[:, j:j + T], wc4[:, j:j + 1], acc, ALU.mult, ALU.add))
                c.se(lambda se: se.activation(xcc, acc, AF.Silu, bias=cbc),
                     st=c.cnt["st"])
                if ch_ == 32:
                    c.ge_dma(lambda ge: ge.dma_start(out=bfm_o, in_=xcc))
                elif ch_ == 33:
                    c.ge_dma(lambda ge: ge.dma_start(out=cfm_o, in_=xcc))
                else:
                    for lt in range(8):
                        c.te(lambda te, lt=lt: te.transpose(
                            psT, xcc[:, lt * 128:(lt + 1) * 128], eye),
                            st=c.cnt["st"])
                        c.se(lambda se: se.copy(xtt, psT), st=c.cnt["st"])
                        c.ge_dma(lambda ge, lt=lt, ch_=ch_: ge.dma_start(
                            out=xct_o[lt * 128:(lt + 1) * 128,
                                      ch_ * 128:(ch_ + 1) * 128], in_=xtt))
                        hb = (ch_ * 128) // 64
                        def x2(ve, lt=lt, hb=hb, ft=None, dst=None):
                            return ve.tensor_tensor(
                                dst.rearrange("p (h w) -> p h w", w=64),
                                xtt.rearrange("p (h w) -> p h w", w=64),
                                ft[:, lt * 64:(lt + 1) * 64]
                                .rearrange("p (h one) -> p h one", one=1)[:, hb:hb + 2, :]
                                .broadcast_to([128, 2, 64]),
                                ALU.mult)
                        c.ve(lambda ve, lt=lt, hb=hb: x2(ve, lt, hb, dtt, x2t),
                             st=c.cnt["st"])
                        c.ge_dma(lambda ge, lt=lt, ch_=ch_: ge.dma_start(
                            out=xst_o[lt * 128:(lt + 1) * 128,
                                      ch_ * 128:(ch_ + 1) * 128], in_=x2t))
                        c.ve(lambda ve, lt=lt, hb=hb: x2(ve, lt, hb, f3t, x3t),
                             st=c.cnt["st"])
                        c.ge_dma(lambda ge, lt=lt, ch_=ch_: ge.dma_start(
                            out=xsdt_o[lt * 128:(lt + 1) * 128,
                                       ch_ * 128:(ch_ + 1) * 128], in_=x3t))

            for ch_ in range(NCH):
                def ldx2(sy, ch_=ch_):
                    out = []
                    for st_ in range(2):
                        out.append(sy.dma_start(
                            out=xs2c[:, st_ * 4096:(st_ + 1) * 4096],
                            in_=xsdt_o[ch_ * CH + st_ * 128: ch_ * CH + (st_ + 1) * 128, :]))
                    return out
                c.sy_dma(ldx2, st=c.cnt["st"], ve=c.cnt["ve"], te=c.cnt["te"])
                def ldb(sy, ch_=ch_):
                    return [sy.dma_start(out=xcc[:, 0:CH],
                                         in_=bfm_o[:, ch_ * CH:(ch_ + 1) * CH])]
                c.sy_dma(ldb, st=c.cnt["st"], se=c.cnt["se"], te=c.cnt["te"])
                for st_ in range(2):
                    c.te(lambda te, st_=st_: te.transpose(
                        psT, xcc[:, st_ * 128:(st_ + 1) * 128], eye), ld=c.cnt["ld"])
                    c.se(lambda se, st_=st_: se.copy(btm[:, st_ * 128:(st_ + 1) * 128], psT))
                c.ge_dma(lambda ge, ch_=ch_: ge.dma_start(
                    out=sinl_o[ch_ * 128:(ch_ + 1) * 128, :], in_=S))
                c.te(lambda te, ch_=ch_: te.matmul(
                    psBC, ones1[0:1, :], erow[0:1, ch_ * 64:(ch_ + 1) * 64],
                    start=True, stop=True))
                c.se(lambda se: se.copy(eab, psBC))
                for rnd in range(2):
                    def smm(te, rnd=rnd):
                        for st_ in range(2):
                            for sl in range(4):
                                ins = te.matmul(
                                    psSt[:, sl * 512:(sl + 1) * 512],
                                    btm[:, st_ * 128:(st_ + 1) * 128],
                                    xs2c[:, st_ * 4096 + rnd * 2048 + sl * 512:
                                         st_ * 4096 + rnd * 2048 + (sl + 1) * 512],
                                    start=(st_ == 0), stop=(st_ == 1))
                        return ins
                    c.te(smm, ld=c.cnt["ld"], st=c.cnt["st"])
                    for sl in range(4):
                        off = rnd * 2048 + sl * 512
                        def smul(ve, off=off):
                            return ve.tensor_tensor(
                                S[:, off:off + 512].rearrange("p (h w) -> p h w", w=64),
                                S[:, off:off + 512].rearrange("p (h w) -> p h w", w=64),
                                eab[:, (off // 64):(off // 64) + 8]
                                .rearrange("p (h one) -> p h one", one=1)
                                .broadcast_to([128, 8, 64]),
                                ALU.mult)
                        c.ve(smul)
                        c.ve(lambda ve, off=off, sl=sl: ve.tensor_add(
                            S[:, off:off + 512], S[:, off:off + 512],
                            psSt[:, sl * 512:(sl + 1) * 512]))
            c.ge_dma(lambda ge: ge.dma_start(out=sloc_o, in_=S), ve=c.cnt["ve"])

            @blk.gpsimd
            def _(ge):
                ge.wait_ge(c.sem["st"], c.cnt["st"])
    return nc


def run_l3b(inputs, l3a_res):
    cw = np.ascontiguousarray(np.asarray(inputs["L1_attn_conv_w"], np.float32)
                              .reshape(4352, 4))
    cb = np.ascontiguousarray(np.asarray(inputs["L1_attn_conv_b"], np.float32))
    dtb = np.ascontiguousarray(np.asarray(inputs["L1_attn_dt_bias"], np.float32))
    alog = np.ascontiguousarray(np.asarray(inputs["L1_attn_A_log"], np.float32))
    eye = np.eye(128, dtype=np.float32)
    nc = build_l3b()
    in_maps = []
    for core in range(NCORES):
        b, hf = core // 2, core % 2
        if hf == 0:
            halo = np.zeros((4352, 3), np.float32)
        else:
            halo = np.ascontiguousarray(l3a_res[2 * b]["xbc_o"][:, -3:])
        in_maps.append(dict(xbc=l3a_res[core]["xbc_o"], halo=halo, cw=cw, cb=cb,
                            dtr=l3a_res[core]["dt_o"], dtb=dtb, alog=alog, eye=eye))
    res = run_bass_kernel_spmd(nc, in_maps, core_ids=list(range(NCORES)))
    return res.results




def build_l4():
    nc = bass.Bass("TRN2", target_bir_lowering=False, debug=False)
    xacc_d = nc.dram_tensor("xacc", [D, T], F32, kind="ExternalInput").ap()
    zt_d = nc.dram_tensor("zt", [T, 4096], F32, kind="ExternalInput").ap()
    xct_d = nc.dram_tensor("xct", [T, 4096], F32, kind="ExternalInput").ap()
    xst_d = nc.dram_tensor("xst", [T, 4096], F32, kind="ExternalInput").ap()
    acs_d = nc.dram_tensor("acs", [64, T], F32, kind="ExternalInput").ap()
    bfm_d = nc.dram_tensor("bfm", [128, T], F32, kind="ExternalInput").ap()
    cfm_d = nc.dram_tensor("cfm", [128, T], F32, kind="ExternalInput").ap()
    sinl_d = nc.dram_tensor("sinl", [NCH * 128, 4096], F32, kind="ExternalInput").ap()
    sinit_d = nc.dram_tensor("sinit", [128, 4096], F32, kind="ExternalInput").ap()
    pfac_d = nc.dram_tensor("pfac", [1, NCH * 64], F32, kind="ExternalInput").ap()
    dvec_d = nc.dram_tensor("dvec", [1, 64], F32, kind="ExternalInput").ap()
    nw_d = nc.dram_tensor("nw", [4096], F32, kind="ExternalInput").ap()
    woutT_d = nc.dram_tensor("woutT", [4096, D], F32, kind="ExternalInput").ap()
    tri_d = nc.dram_tensor("tri", [CH, CH], F32, kind="ExternalInput").ap()
    eye_d = nc.dram_tensor("eye", [128, 128], F32, kind="ExternalInput").ap()
    out_o = nc.dram_tensor("out_o", [D, T], F32, kind="ExternalOutput").ap()

    with ExitStack() as es:
        sbm = lambda name, shape, dt=F32: es.enter_context(nc.sbuf_tensor(name, shape, dt))[:]
        psm = lambda name, shape, dt=F32: es.enter_context(nc.psum_tensor(name, shape, dt))[:]

        xsc = sbm("xsc", [128, 2 * 4096])
        sinit = sbm("sinit_s", [128, 4096])
        stru = sbm("stru", [128, 4096])
        xcl = sbm("xcl", [128, 4096])
        bufA = sbm("bufA", [128, 4096])
        scr4 = sbm("scr4", [128, 4096])
        ybuf = sbm("ybuf", [128, 2 * 4096])
        wvo = sbm("wvo", [128, 16 * 128])
        bfmc = sbm("bfmc", [128, T])
        cfmc = sbm("cfmc", [128, T])
        acs_s = sbm("acs_s", [64, T])
        acsr = sbm("acsr", [1, 8 * CH])
        acst = sbm("acst", [128, 2 * 64])
        eat = sbm("eat", [128, 2 * 64])
        trib = sbm("trib", [128, 2 * CH])
        cbsm = sbm("cbsm", [128, 2 * CH])
        dbuf = sbm("dbuf", [128, 2 * CH])
        ebuf = sbm("ebuf", [128, 2 * CH])
        pf128 = sbm("pf128", [128, 64])
        d128 = sbm("d128", [128, 64])
        nw32 = sbm("nw32", [128, 32])
        dro = sbm("dro", [1, 64])
        pfr = sbm("pfr", [1, NCH * 64])
        tmp5 = sbm("tmp5", [128, 512])
        s1 = sbm("s1", [128, 1])
        ms = sbm("ms", [128, 1])
        rstd1 = sbm("rstd1", [128, 1])
        xaccl = sbm("xaccl", [128, 16 * 128])
        cpool = sbm("cpool", [128, 2])
        ones1 = sbm("ones1", [1, 128])
        eye = sbm("eye_s", [128, 128])

        psY = [psm("psY0", [128, 512]), psm("psY1", [128, 512])]
        psY2 = [psm("psY2a", [128, 512]), psm("psY2b", [128, 512])]
        psCB = psm("psCB", [128, 512])
        psRB = psm("psRB", [128, CH])
        psT = psm("psT", [128, 128])
        psMISC = psm("psMISC", [128, 512])

        with nc.Block() as blk:
            c = Ctx(nc, es, blk)

            def ld0(sy):
                return [
                    sy.dma_start(out=eye, in_=eye_d),
                    sy.dma_start(out=bfmc, in_=bfm_d),
                    sy.dma_start(out=cfmc, in_=cfm_d),
                    sy.dma_start(out=acs_s, in_=acs_d),
                    sy.dma_start(out=sinit, in_=sinit_d),
                    sy.dma_start(out=pfr, in_=pfac_d),
                    sy.dma_start(out=dro, in_=dvec_d),
                    sy.dma_start(out=trib.rearrange("p (st l) -> p st l", l=CH),
                                 in_=tri_d.rearrange("(st p) l -> p st l", p=128)),
                ]
            c.sy_dma(ld0)
            c.ve(lambda ve: ve.memset(cpool[:, 0:1], 1e-5))
            c.ve(lambda ve: ve.memset(ones1, 1.0))

            c.te(lambda te: te.matmul(psMISC[:, 0:64], ones1[0:1, :], dro[0:1, :],
                                      start=True, stop=True), ld=c.cnt["ld"])
            c.se(lambda se: se.copy(d128, psMISC[:, 0:64]))
            for ic in range(32):
                c.sy_dma(lambda sy, ic=ic: [sy.dma_start(
                    out=nw32[:, ic:ic + 1],
                    in_=nw_d[ic * 128:(ic + 1) * 128].rearrange("(p one) -> p one", one=1))])

            for ch_ in range(NCH):
                def ldc(sy, ch_=ch_):
                    out = [sy.dma_start(out=xsc[:, st_ * 4096:(st_ + 1) * 4096],
                                        in_=xst_d[ch_ * CH + st_ * 128:
                                                  ch_ * CH + (st_ + 1) * 128, :])
                           for st_ in range(2)]
                    out.append(sy.dma_start(out=stru, in_=sinl_d[ch_ * 128:(ch_ + 1) * 128, :]))
                    return out
                c.sy_dma(ldc, te=c.cnt["te"], ve=c.cnt["ve"], se=c.cnt["se"])

                c.te(lambda te, ch_=ch_: te.matmul(
                    psMISC[:, 0:64], ones1[0:1, :], pfr[0:1, ch_ * 64:(ch_ + 1) * 64],
                    start=True, stop=True), ld=c.cnt["ld"])
                c.se(lambda se: se.copy(pf128, psMISC[:, 0:64]))
                for sl in range(8):
                    off = sl * 512
                    def st1(ve, off=off):
                        return ve.tensor_tensor(
                            tmp5.rearrange("p (h w) -> p h w", w=64),
                            sinit[:, off:off + 512].rearrange("p (h w) -> p h w", w=64),
                            pf128[:, off // 64:off // 64 + 8]
                            .rearrange("p (h one) -> p h one", one=1)
                            .broadcast_to([128, 8, 64]), ALU.mult)
                    c.ve(st1, ld=c.cnt["ld"])
                    c.ve(lambda ve, off=off: ve.tensor_add(
                        stru[:, off:off + 512], stru[:, off:off + 512], tmp5))

                for st_ in range(2):
                    c.te(lambda te, ch_=ch_, st_=st_: te.transpose(
                        psT[:, 0:64],
                        acs_s[0:64, ch_ * CH + st_ * 128: ch_ * CH + (st_ + 1) * 128],
                        eye[0:64, 0:64]))
                    c.se(lambda se, st_=st_: se.copy(acst[:, st_ * 64:(st_ + 1) * 64],
                                                     psT[:, 0:64]))
                    c.se(lambda se, st_=st_: se.activation(
                        eat[:, st_ * 64:(st_ + 1) * 64], psT[:, 0:64], AF.Exp))

                for st_ in range(2):
                    c.te(lambda te, ch_=ch_, st_=st_: te.matmul(
                        psCB[:, st_ * CH:(st_ + 1) * CH],
                        bfmc[:, ch_ * CH + st_ * 128: ch_ * CH + (st_ + 1) * 128],
                        cfmc[:, ch_ * CH:(ch_ + 1) * CH],
                        start=True, stop=True))
                c.ve(lambda ve: ve.tensor_mul(cbsm, psCB, trib))

                for hg in range(8):
                    def ldacs(sy, hg=hg, ch_=ch_):
                        return [sy.dma_start(
                            out=acsr[0:1, h8_ * CH:(h8_ + 1) * CH],
                            in_=acs_d[hg * 8 + h8_, ch_ * CH:(ch_ + 1) * CH]
                            .rearrange("(one l) -> one l", one=1)) for h8_ in range(8)]
                    c.sy_dma(ldacs, te=c.cnt["te"])
                    for h8 in range(8):
                        h = hg * 8 + h8
                        c.te(lambda te, h8=h8: te.matmul(
                            psRB, ones1[0:1, :], acsr[0:1, h8 * CH:(h8 + 1) * CH],
                            start=True, stop=True), ld=c.cnt["ld"])
                        for st_ in range(2):
                            c.ve(lambda ve, st_=st_, h=h: ve.tensor_scalar(
                                dbuf[:, st_ * CH:(st_ + 1) * CH], psRB,
                                acst[:, st_ * 64 + h:st_ * 64 + h + 1], 0.0,
                                ALU.subtract, ALU.min))
                        c.se(lambda se: se.activation(ebuf, dbuf, AF.Exp))
                        c.ve(lambda ve: ve.tensor_mul(ebuf, ebuf, cbsm))
                        for lt in range(2):
                            def ymm(te, h=h, h8=h8, lt=lt, ch_=ch_):
                                te.matmul(psY2[lt][:, h8 * 64:(h8 + 1) * 64],
                                          cfmc[:, ch_ * CH + lt * 128:
                                               ch_ * CH + (lt + 1) * 128],
                                          stru[:, h * 64:(h + 1) * 64],
                                          start=True, stop=True)
                                for st_ in range(2):
                                    ins = te.matmul(
                                        psY[lt][:, h8 * 64:(h8 + 1) * 64],
                                        ebuf[:, st_ * CH + lt * 128:
                                             st_ * CH + (lt + 1) * 128],
                                        xsc[:, st_ * 4096 + h * 64:
                                            st_ * 4096 + (h + 1) * 64],
                                        start=(st_ == 0), stop=(st_ == 1))
                                return ins
                            c.te(ymm)
                    for lt in range(2):
                        def yevac(ve, lt=lt, hg=hg):
                            return ve.tensor_tensor(
                                ybuf[:, lt * 4096 + hg * 512: lt * 4096 + (hg + 1) * 512]
                                .rearrange("p (h w) -> p h w", w=64),
                                psY2[lt][:, :].rearrange("p (h w) -> p h w", w=64),
                                eat[:, lt * 64 + hg * 8: lt * 64 + (hg + 1) * 8]
                                .rearrange("p (h one) -> p h one", one=1)
                                .broadcast_to([128, 8, 64]), ALU.mult)
                        c.ve(yevac)
                        c.ve(lambda ve, lt=lt, hg=hg: ve.tensor_add(
                            ybuf[:, lt * 4096 + hg * 512: lt * 4096 + (hg + 1) * 512],
                            ybuf[:, lt * 4096 + hg * 512: lt * 4096 + (hg + 1) * 512],
                            psY[lt][:, :]))

                for lt in range(2):
                    tok0 = ch_ * CH + lt * 128
                    def ldlt(sy, tok0=tok0):
                        return [
                            sy.dma_start(out=xcl, in_=xct_d[tok0:tok0 + 128, :]),
                            sy.dma_start(out=bufA, in_=zt_d[tok0:tok0 + 128, :]),
                        ]
                    c.sy_dma(ldlt, te=c.cnt["te"], ve=c.cnt["ve"], se=c.cnt["se"],
                             st=c.cnt["st"])
                    ysl = ybuf[:, lt * 4096:(lt + 1) * 4096]
                    def xcd(ve):
                        return ve.tensor_tensor(
                            xcl.rearrange("p (h w) -> p h w", w=64),
                            xcl.rearrange("p (h w) -> p h w", w=64),
                            d128.rearrange("p (h one) -> p h one", one=1)
                            .broadcast_to([128, 64, 64]), ALU.mult)
                    c.ve(xcd, ld=c.cnt["ld"])
                    c.ve(lambda ve, ysl=ysl: ve.tensor_add(ysl, ysl, xcl))
                    c.se(lambda se: se.activation(scr4, bufA, AF.Silu), ld=c.cnt["ld"])
                    c.ve(lambda ve, ysl=ysl: ve.tensor_mul(ysl, ysl, scr4))
                    c.se(lambda se, ysl=ysl: se.activation(scr4, ysl, AF.Square,
                                                           accum_out=s1))
                    c.ve(lambda ve: ve.tensor_scalar(ms, s1, 1.0 / 4096.0, None, ALU.mult))
                    c.se(lambda se: se.activation(ms, ms, AF.Sqrt, bias=cpool[:, 0:1]))
                    c.ve(lambda ve: ve.reciprocal(rstd1, ms))
                    c.ve(lambda ve, ysl=ysl: ve.tensor_scalar(ysl, ysl, rstd1,
                                                              None, ALU.mult))
                    for ic in range(32):
                        c.te(lambda te, ysl=ysl, ic=ic: te.transpose(
                            psT, ysl[:, ic * 128:(ic + 1) * 128], eye))
                        c.ve(lambda ve, ic=ic: ve.tensor_scalar(
                            bufA[:, ic * 128:(ic + 1) * 128], psT,
                            nw32[:, ic:ic + 1], None, ALU.mult))
                    def ldxa(sy, tok0=tok0):
                        out = []
                        for oc in range(16):
                            out.append(sy.dma_start(
                                out=xaccl[:, oc * 128:(oc + 1) * 128],
                                in_=xacc_d[oc * 128:(oc + 1) * 128, tok0:tok0 + 128]))
                        return out
                    c.sy_dma(ldxa, st=c.cnt["st"], ve=c.cnt["ve"])
                    for oc in range(16):
                        for half in range(2):
                            c.sy_dma(lambda sy, oc=oc, half=half: [sy.dma_start(
                                out=wvo.rearrange("p (ic g) -> p ic g", g=128),
                                in_=woutT_d[half * 2048:(half + 1) * 2048,
                                            oc * 128:(oc + 1) * 128].rearrange(
                                    "(ic p) g -> p ic g", p=128))], te=c.cnt["te"])
                            def wmm(te, half=half):
                                for ic in range(16):
                                    ins = te.matmul(
                                        psMISC[:, 0:128], wvo[:, ic * 128:(ic + 1) * 128],
                                        bufA[:, half * 2048 + ic * 128:
                                             half * 2048 + (ic + 1) * 128],
                                        start=(half == 0 and ic == 0),
                                        stop=(half == 1 and ic == 15))
                                return ins
                            c.te(wmm, ld=c.cnt["ld"])
                        c.ve(lambda ve, oc=oc: ve.scalar_tensor_tensor(
                            xaccl[:, oc * 128:(oc + 1) * 128], psMISC[:, 0:128], 1.0,
                            xaccl[:, oc * 128:(oc + 1) * 128], ALU.mult, ALU.add))
                    def stx(ge, tok0=tok0):
                        return ge.dma_start(
                            out=out_o.rearrange("(oc p) t -> p oc t", p=128)
                            [:, :, tok0:tok0 + 128],
                            in_=xaccl.rearrange("p (oc g) -> p oc g", g=128))
                    c.ge_dma(stx, ve=c.cnt["ve"])

            @blk.gpsimd
            def _(ge):
                ge.wait_ge(c.sem["st"], c.cnt["st"])
    return nc


def run_l4(inputs, l3a_res, l3b_res):
    woutT = np.ascontiguousarray(np.asarray(inputs["L1_attn_W_out"], np.float32).T)
    nw = np.ascontiguousarray(np.asarray(inputs["L1_attn_norm_w"], np.float32))
    dvec = np.ascontiguousarray(np.asarray(inputs["L1_attn_D"], np.float32)[None, :])
    ss, ll = np.meshgrid(np.arange(CH), np.arange(CH), indexing="ij")
    tri = (ss <= ll).astype(np.float32)
    eye = np.eye(128, dtype=np.float32)
    nc = build_l4()
    in_maps = []
    for core in range(NCORES):
        b, hf = core // 2, core % 2
        if hf == 0:
            sinit = np.zeros((128, 4096), np.float32)
        else:
            sinit = l3b_res[2 * b]["sloc_o"]
        zt = np.ascontiguousarray(l3a_res[core]["z_o"].T)
        in_maps.append(dict(
            xacc=l3a_res[core]["xacc_o"], zt=zt, xct=l3b_res[core]["xct_o"],
            xst=l3b_res[core]["xst_o"], acs=l3b_res[core]["acs_o"],
            bfm=l3b_res[core]["bfm_o"], cfm=l3b_res[core]["cfm_o"],
            sinl=l3b_res[core]["sinl_o"], sinit=sinit,
            pfac=l3b_res[core]["pfac_o"], dvec=dvec, nw=nw, woutT=woutT,
            tri=tri, eye=eye))
    res = run_bass_kernel_spmd(nc, in_maps, core_ids=list(range(NCORES)))
    return res.results




TAPS = False


def build_fused():
    nc = bass.Bass("TRN2", target_bir_lowering=False, debug=False)
    x0_d = nc.dram_tensor("x0", [D, T], F32, kind="ExternalInput").ap()
    wqkv_d = nc.dram_tensor("wqkv", [D, 3 * D], F32, kind="ExternalInput").ap()
    bqkv_d = nc.dram_tensor("bqkv", [3 * D], F32, kind="ExternalInput").ap()
    gwT_d = nc.dram_tensor("gwT", [D, INTER], F32, kind="ExternalInput").ap()
    uwT_d = nc.dram_tensor("uwT", [D, INTER], F32, kind="ExternalInput").ap()
    dwT_d = nc.dram_tensor("dwT", [INTER, D], F32, kind="ExternalInput").ap()
    gs_d = nc.dram_tensor("gs", [INTER], F32, kind="ExternalInput").ap()
    us_d = nc.dram_tensor("us", [INTER], F32, kind="ExternalInput").ap()
    ds_d = nc.dram_tensor("ds", [D], F32, kind="ExternalInput").ap()
    cos_d = nc.dram_tensor("cosx", [ROT, T], F32, kind="ExternalInput").ap()
    sin_d = nc.dram_tensor("sinx", [ROT, T], F32, kind="ExternalInput").ap()
    p32_d = nc.dram_tensor("p32", [ROT, ROT], F32, kind="ExternalInput").ap()
    eye_d = nc.dram_tensor("eye", [128, 128], F32, kind="ExternalInput").ap()
    mask_d = nc.dram_tensor("mask", [2 * T, T], F32, kind="ExternalInput").ap()
    wden_d = nc.dram_tensor("wden", [D, D], F32, kind="ExternalInput").ap()
    bden_d = nc.dram_tensor("bden", [D], F32, kind="ExternalInput").ap()
    winT_d = nc.dram_tensor("winT", [D, 8512], F32, kind="ExternalInput").ap()
    wT_d = nc.dram_tensor("wT", [D, D], F32, kind="ExternalInput").ap()
    aT_d = nc.dram_tensor("aT", [D, 256], F32, kind="ExternalInput").ap()
    rw1T_d = nc.dram_tensor("rw1T", [D, 64], F32, kind="ExternalInput").ap()
    rw2T_d = nc.dram_tensor("rw2T", [64, 32], F32, kind="ExternalInput").ap()
    rb1_d = nc.dram_tensor("rb1", [64], F32, kind="ExternalInput").ap()
    rb2_d = nc.dram_tensor("rb2", [16], F32, kind="ExternalInput").ap()
    gates_d = nc.dram_tensor("gates", [16], F32, kind="ExternalInput").ap()
    bias_d = nc.dram_tensor("biasW", [D], F32, kind="ExternalInput").ap()
    bflat_d = nc.dram_tensor("bflat", [256, D], F32, kind="ExternalInput").ap()
    e16_d = nc.dram_tensor("e16", [16, 256], F32, kind="ExternalInput").ap()
    cw_d = nc.dram_tensor("cw", [4352, 4], F32, kind="ExternalInput").ap()
    cb_d = nc.dram_tensor("cb", [4352], F32, kind="ExternalInput").ap()
    dtb_d = nc.dram_tensor("dtb", [64], F32, kind="ExternalInput").ap()
    alog_d = nc.dram_tensor("alog", [64], F32, kind="ExternalInput").ap()
    dvec_d = nc.dram_tensor("dvec", [1, 64], F32, kind="ExternalInput").ap()
    nw_d = nc.dram_tensor("nw", [4096], F32, kind="ExternalInput").ap()
    woutT_d = nc.dram_tensor("woutT", [4096, D], F32, kind="ExternalInput").ap()
    tri_d = nc.dram_tensor("tri", [CH, CH], F32, kind="ExternalInput").ap()
    hm_d = nc.dram_tensor("hm", [128, 1], F32, kind="ExternalInput").ap()
    out_o = nc.dram_tensor("out_o", [D, T], F32, kind="ExternalOutput").ap()

    kown_h = nc.dram_tensor("kown", [D, T], F32, kind="Internal")
    vown_h = nc.dram_tensor("vown", [T, D], F32, kind="Internal")
    qkm_h = nc.dram_tensor("qkmi", [1, 2], F32, kind="Internal")
    q_i = nc.dram_tensor("q_i", [D, T], F32, kind="ExternalOutput" if TAPS else "Internal").ap()
    kf_h = nc.dram_tensor("kf_i", [2 * D, T], F32, kind="Internal")
    vf_h = nc.dram_tensor("vf_i", [2 * T, D], F32, kind="Internal")
    qkm2_h = nc.dram_tensor("qkm2", [2, 2], F32, kind="Internal")
    z_i = nc.dram_tensor("z_i", [4096, T], F32, kind="ExternalOutput" if TAPS else "Internal").ap()
    xbc_i = nc.dram_tensor("xbc_i", [4352, T], F32, kind="ExternalOutput" if TAPS else "Internal").ap()
    dtr_i = nc.dram_tensor("dtr_i", [64, T], F32, kind="ExternalOutput" if TAPS else "Internal").ap()
    xacc2_i = nc.dram_tensor("xacc2", [D, T], F32, kind="ExternalOutput" if TAPS else "Internal").ap()
    xbtail_h = nc.dram_tensor("xbtail", [4352, 4], F32, kind="Internal")
    halo_h = nc.dram_tensor("halo_i", [2 * 4352, 4], F32, kind="Internal")
    xct_i = nc.dram_tensor("xct_i", [T, 4096], F32, kind="ExternalOutput" if TAPS else "Internal").ap()
    xst_i = nc.dram_tensor("xst_i", [T, 4096], F32, kind="ExternalOutput" if TAPS else "Internal").ap()
    xsdt_i = nc.dram_tensor("xsdt_i", [T, 4096], F32, kind="ExternalOutput" if TAPS else "Internal").ap()
    acs_i = nc.dram_tensor("acs_i", [64, T], F32, kind="ExternalOutput" if TAPS else "Internal").ap()
    bfm_i = nc.dram_tensor("bfm_i", [128, T], F32, kind="ExternalOutput" if TAPS else "Internal").ap()
    cfm_i = nc.dram_tensor("cfm_i", [128, T], F32, kind="ExternalOutput" if TAPS else "Internal").ap()
    sinl_i = nc.dram_tensor("sinl_i", [NCH * 128, 4096], F32, kind="ExternalOutput" if TAPS else "Internal").ap()
    sloc_h = nc.dram_tensor("sloc_i", [128, 4096], F32, kind="Internal")
    sgth_h = nc.dram_tensor("sgth", [2 * 128, 4096], F32, kind="Internal")
    pfac_i = nc.dram_tensor("pfac_i", [1, NCH * 64], F32, kind="ExternalOutput" if TAPS else "Internal").ap()

    RG = [[0, 1], [2, 3], [4, 5], [6, 7]]
    NKT = (2 * T) // 128

    with ExitStack() as es0:
        sbm0 = lambda name, shape, dt=F32: es0.enter_context(nc.sbuf_tensor(name, shape, dt))[:]
        ones128 = sbm0("ones128", [128, 1])
        ones1 = sbm0("ones1", [1, 128])
        eye = sbm0("eye_s", [128, 128])
        cpool = sbm0("cpool", [128, 4])
        hm = sbm0("hm_s", [128, 1])
        esx = ExitStack()
        xacc = esx.enter_context(nc.sbuf_tensor("xacc", [128, KC * T], F32))[:]

        with nc.Block() as blk:
            c = Ctx(nc, es0, blk)
            c.sem["cc"] = es0.enter_context(nc.semaphore("cc"))
            c.cnt["cc"] = 0

            def ld00(sy):
                out = [sy.dma_start(out=eye, in_=eye_d),
                       sy.dma_start(out=hm, in_=hm_d)]
                for kc in range(KC):
                    out.append(sy.dma_start(out=xacc[:, kc * T:(kc + 1) * T],
                                            in_=x0_d[kc * 128:(kc + 1) * 128, :]))
                return out
            c.sy_dma(ld00)
            c.ve(lambda ve: ve.memset(ones128, 1.0))
            c.ve(lambda ve: ve.memset(ones1, 1.0))
            c.ve(lambda ve: ve.memset(cpool[:, 0:1], 1e-5))
            c.ve(lambda ve: ve.memset(cpool[:, 1:2], 1.0))

            with ExitStack() as es:
                sbm = lambda name, shape, dt=F32: es.enter_context(nc.sbuf_tensor(name, shape, dt))[:]
                psml = lambda name, shape, dt=F32: es.enter_context(nc.psum_tensor(name, shape, dt))[:]
                xn = sbm("p0xn", [128, KC * TS])
                sq = sbm("p0sq", [128, TS])
                xtmp = sbm("p0xtmp", [128, TS])
                mb = sbm("p0mb", [128, TS])
                rb = sbm("p0rb", [128, TS])
                stat = sbm("p0stat", [1, 8 * TS])
                cosx = sbm("p0cos", [ROT, T])
                sinx = sbm("p0sin", [ROT, T])
                p32 = sbm("p0p32", [ROT, ROT])
                wq = [sbm("p0wq0", [128, D]), sbm("p0wq1", [128, D])]
                bias = sbm("p0bias", [128, 2])
                gt = [sbm("p0gt0", [128, TS]), sbm("p0gt1", [128, TS])]
                r1 = sbm("p0r1", [ROT, TS])
                r2 = sbm("p0r2", [ROT, TS])
                nm1 = sbm("p0nm1", [1, 1])
                runm = sbm("p0runm", [1, 2])
                vt_t = sbm("p0vt", [128, 128])
                wg = sbm("p0wg", [128, D])
                wu = sbm("p0wu", [128, D])
                wd = sbm("p0wd", [128, IGRP * 128])
                gsb = sbm("p0gsb", [128, 2])
                dsb = sbm("p0dsb", [128, 16])
                tg = sbm("p0tg", [128, TS])
                tu = sbm("p0tu", [128, TS])
                hsb = sbm("p0hsb", [128, IGRP * TS])
                psA = psml("p0psA", [128, 512])
                psB = psml("p0psB", [128, 512])
                psR1 = psml("p0psR1", [1, 512])
                psR2 = psml("p0psR2", [1, 512])
                psBC = psml("p0psBC", [128, 512])
                psT = psml("p0psT", [128, 128])
                psP = psml("p0psP", [ROT, 512])

                def ldx(sy):
                    out = [sy.dma_start(out=cosx, in_=cos_d),
                           sy.dma_start(out=sinx, in_=sin_d),
                           sy.dma_start(out=p32, in_=p32_d)]
                    for oc in range(16):
                        out.append(sy.dma_start(
                            out=dsb[:, oc:oc + 1],
                            in_=ds_d[oc * 128:(oc + 1) * 128].rearrange("(p one) -> p one", one=1)))
                    return out
                c.sy_dma(ldx)
                c.ve(lambda ve: ve.memset(runm, 0.0))

                for s in range(NSTRIP):
                    def xs(kc, s=s):
                        return xacc[:, kc * T + s * TS: kc * T + s * TS + TS]
                    for kc in range(KC):
                        c.se(lambda se, kc=kc: se.activation(sq, xs(kc), AF.Square),
                             ld=c.cnt["ld"])
                        c.te(lambda te, kc=kc: te.matmul(psR1[0:1, :], ones128, xs(kc),
                                                         start=(kc == 0), stop=(kc == KC - 1)))
                        c.te(lambda te, kc=kc: te.matmul(psR2[0:1, :], ones128, sq,
                                                         start=(kc == 0), stop=(kc == KC - 1)))
                    m = stat[0:1, 0:TS]; e2 = stat[0:1, TS:2 * TS]
                    var = stat[0:1, 2 * TS:3 * TS]; sd = stat[0:1, 3 * TS:4 * TS]
                    rstd = stat[0:1, 4 * TS:5 * TS]
                    c.ve(lambda ve: ve.tensor_scalar(m, psR1[0:1, :], 1.0 / D, None, ALU.mult))
                    c.ve(lambda ve: ve.tensor_scalar(e2, psR2[0:1, :], 1.0 / D, None, ALU.mult))
                    c.ve(lambda ve: ve.tensor_tensor(var, m, m, ALU.mult))
                    c.ve(lambda ve: ve.tensor_sub(var, e2, var))
                    c.se(lambda se: se.activation(sd, var, AF.Sqrt, bias=cpool[0:1, 0:1]))
                    c.ve(lambda ve: ve.reciprocal(rstd, sd))
                    c.te(lambda te: te.matmul(psBC, ones1[0:1, :], m, start=True, stop=True))
                    c.se(lambda se: se.copy(mb, psBC))
                    c.te(lambda te: te.matmul(psBC, ones1[0:1, :], rstd, start=True, stop=True))
                    c.se(lambda se: se.copy(rb, psBC))
                    for kc in range(KC):
                        c.ve(lambda ve, kc=kc: ve.tensor_sub(xtmp, xs(kc), mb))
                        c.ve(lambda ve, kc=kc: ve.tensor_mul(xn[:, kc * TS:(kc + 1) * TS], xtmp, rb))

                    for h in range(NH0):
                        for r in range(3):
                            gcol = 384 * h + 128 * r
                            idx = h * 3 + r
                            wbuf = wq[idx % 2]
                            gbuf = gt[idx % 2]
                            bcol = bias[:, idx % 2:idx % 2 + 1]
                            c.sy_dma(lambda sy, gcol=gcol, wbuf=wbuf, bcol=bcol: [
                                sy.dma_start(out=wbuf.rearrange("p (kc g) -> p kc g", g=128),
                                             in_=wqkv_d[:, gcol:gcol + 128].rearrange(
                                                 "(kc p) g -> p kc g", p=128)),
                                sy.dma_start(out=bcol,
                                             in_=bqkv_d[gcol:gcol + 128].rearrange(
                                                 "(p one) -> p one", one=1)),
                            ], te=c.cnt["te"], se=c.cnt["se"])

                            def qkv_mm(te, wbuf=wbuf):
                                for kc in range(KC):
                                    ins = te.matmul(psA, wbuf[:, kc * 128:(kc + 1) * 128],
                                                    xn[:, kc * TS:(kc + 1) * TS],
                                                    start=(kc == 0), stop=(kc == KC - 1))
                                return ins
                            c.te(qkv_mm, ld=c.cnt["ld"])
                            c.se(lambda se, gbuf=gbuf, bcol=bcol:
                                 se.activation(gbuf, psA, AF.Identity, bias=bcol),
                                 st=c.cnt["st"])
                            if r < 2:
                                sl = slice(s * TS, (s + 1) * TS)
                                c.te(lambda te, gbuf=gbuf: te.matmul(
                                    psP, p32, gbuf[0:ROT, :], start=True, stop=True))
                                c.ve(lambda ve, sl=sl: ve.tensor_tensor(
                                    r1, psP[:, :], sinx[:, sl], ALU.mult))
                                c.ve(lambda ve, gbuf=gbuf, sl=sl: ve.tensor_tensor(
                                    r2, gbuf[0:ROT, :], cosx[:, sl], ALU.mult))
                                c.ve(lambda ve, gbuf=gbuf: ve.tensor_add(gbuf[0:ROT, :], r1, r2))
                                c.se(lambda se, gbuf=gbuf: se.activation(sq, gbuf, AF.Square))
                                c.te(lambda te: te.matmul(psR1[0:1, :], ones128, sq,
                                                          start=True, stop=True))
                                c.ve(lambda ve: ve.reduce_max(nm1, psR1[0:1, :], axis=AX.X))
                                c.ve(lambda ve, r=r: ve.tensor_max(
                                    runm[0:1, r:r + 1], runm[0:1, r:r + 1], nm1))
                                c.ge_dma(lambda ge, h=h, r=r, gbuf=gbuf, s=s: ge.dma_start(
                                    out=(q_i if r == 0 else kown_h.ap())[
                                        h * 128:(h + 1) * 128, s * TS:(s + 1) * TS],
                                    in_=gbuf))
                            else:
                                for j in range(4):
                                    c.te(lambda te, gbuf=gbuf, j=j: te.transpose(
                                        psT, gbuf[:, j * 128:(j + 1) * 128], eye))
                                    c.se(lambda se: se.copy(vt_t, psT), st=c.cnt["st"])
                                    c.ge_dma(lambda ge, h=h, j=j, s=s: ge.dma_start(
                                        out=vown_h.ap()[s * TS + j * 128: s * TS + (j + 1) * 128,
                                                        h * 128:(h + 1) * 128],
                                        in_=vt_t))

                    for ig in range(NIG):
                        for ic in range(IGRP):
                            i0 = (ig * IGRP + ic) * 128
                            c.sy_dma(lambda sy, i0=i0: [
                                sy.dma_start(out=wg.rearrange("p (kc g) -> p kc g", g=128),
                                             in_=gwT_d[:, i0:i0 + 128].rearrange(
                                                 "(kc p) g -> p kc g", p=128)),
                                sy.dma_start(out=wu.rearrange("p (kc g) -> p kc g", g=128),
                                             in_=uwT_d[:, i0:i0 + 128].rearrange(
                                                 "(kc p) g -> p kc g", p=128)),
                                sy.dma_start(out=gsb[:, 0:1], in_=gs_d[i0:i0 + 128].rearrange(
                                    "(p one) -> p one", one=1)),
                                sy.dma_start(out=gsb[:, 1:2], in_=us_d[i0:i0 + 128].rearrange(
                                    "(p one) -> p one", one=1)),
                            ], te=c.cnt["te"], se=c.cnt["se"])

                            def gu_mm(te):
                                for kc in range(KC):
                                    te.matmul(psA, wg[:, kc * 128:(kc + 1) * 128],
                                              xn[:, kc * TS:(kc + 1) * TS],
                                              start=(kc == 0), stop=(kc == KC - 1))
                                for kc in range(KC):
                                    ins = te.matmul(psB, wu[:, kc * 128:(kc + 1) * 128],
                                                    xn[:, kc * TS:(kc + 1) * TS],
                                                    start=(kc == 0), stop=(kc == KC - 1))
                                return ins
                            c.te(gu_mm, ld=c.cnt["ld"])
                            c.se(lambda se: se.activation(tg, psA, AF.Silu, scale=gsb[:, 0:1]))
                            c.se(lambda se: se.activation(tu, psB, AF.Identity, scale=gsb[:, 1:2]))
                            c.ve(lambda ve, ic=ic: ve.tensor_tensor(
                                hsb[:, ic * TS:(ic + 1) * TS], tg, tu, ALU.mult))
                        for oc in range(16):
                            c.sy_dma(lambda sy, ig=ig, oc=oc: [
                                sy.dma_start(out=wd.rearrange("p (ic o) -> p ic o", o=128),
                                             in_=dwT_d[ig * IGRP * 128:(ig + 1) * IGRP * 128,
                                                       oc * 128:(oc + 1) * 128].rearrange(
                                                 "(ic p) o -> p ic o", p=128)),
                            ], te=c.cnt["te"])

                            def dn_mm(te):
                                for ic in range(IGRP):
                                    ins = te.matmul(psA, wd[:, ic * 128:(ic + 1) * 128],
                                                    hsb[:, ic * TS:(ic + 1) * TS],
                                                    start=(ic == 0), stop=(ic == IGRP - 1))
                                return ins
                            c.te(dn_mm, ld=c.cnt["ld"])

                            def dn_acc(ve, oc=oc, s=s):
                                xsl = xacc[:, oc * T + s * TS: oc * T + s * TS + TS]
                                return ve.scalar_tensor_tensor(
                                    xsl, psA, dsb[:, oc:oc + 1], xsl, ALU.mult, ALU.add)
                            c.ve(dn_acc)

                c.ge_dma(lambda ge: ge.dma_start(out=qkm_h.ap(), in_=runm),
                         ve=c.cnt["ve"])

                @blk.gpsimd
                def _(ge):
                    ge.wait_ge(c.sem["st"], c.cnt["st"])
                    for src_h, dst_h in [(kown_h, kf_h), (vown_h, vf_h), (qkm_h, qkm2_h)]:
                        ins_ = ge.collective_compute(
                            "AllGather", ALU.bypass, replica_groups=RG,
                            ins=[src_h.ap().opt()], outs=[dst_h.ap().opt()])
                        c.i("cc", ins_, 1)
            cc_kv = c.cnt["cc"]

            with ExitStack() as es:
                sbm = lambda name, shape, dt=F32: es.enter_context(nc.sbuf_tensor(name, shape, dt))[:]
                psml = lambda name, shape, dt=F32: es.enter_context(nc.psum_tensor(name, shape, dt))[:]
                qh_s = sbm("p2qh", [128, KC * TS])
                mask_s = sbm("p2mask", [128, NKT * TS])
                kbuf = sbm("p2kbuf", [128, 2 * T])
                vbuf = sbm("p2vbuf", [128, NKT * 128])
                expb = [sbm("p2e0", [128, TS]), sbm("p2e1", [128, TS])]
                attn_s = sbm("p2attn", [128, NH0 * TS])
                rbc = sbm("p2rbc", [128, TS])
                wdn = sbm("p2wdn", [128, D])
                bdn = sbm("p2bdn", [128, 16])
                qkm = sbm("p2qkm", [1, 4])
                t11 = sbm("p2t11", [1, 2])
                c0b = sbm("p2c0b", [128, 1])
                psS = psml("p2psS", [128, 512])
                psPV = psml("p2psPV", [128, 512])
                psR = psml("p2psR", [1, 512])
                psBC2 = psml("p2psBC", [128, 512])
                psD = psml("p2psD", [128, 512])

                def ld20(sy):
                    out = [sy.dma_start(out=qkm[0:1, 0:2], in_=qkm_h.ap()),
                           sy.dma_start(out=qkm[0:1, 2:3], in_=qkm2_h.ap()[0:1, 1:2]),
                           sy.dma_start(out=qkm[0:1, 3:4], in_=qkm2_h.ap()[1:2, 1:2])]
                    for oc in range(16):
                        out.append(sy.dma_start(
                            out=bdn[:, oc:oc + 1],
                            in_=bden_d[oc * 128:(oc + 1) * 128].rearrange("(p one) -> p one", one=1)))
                    return out
                c.sy_dma(ld20, cc=cc_kv)

                c.ve(lambda ve: ve.tensor_max(t11[0:1, 0:1], qkm[0:1, 2:3], qkm[0:1, 3:4]),
                     ld=c.cnt["ld"])
                c.ve(lambda ve: ve.tensor_mul(t11[0:1, 0:1], t11[0:1, 0:1], qkm[0:1, 0:1]))
                c.se(lambda se: se.activation(t11[0:1, 1:2], t11[0:1, 0:1], AF.Sqrt))
                c.ve(lambda ve: ve.tensor_scalar(t11[0:1, 0:1], t11[0:1, 1:2],
                                                 -1.0 / math.sqrt(HD), None, ALU.mult))
                c.te(lambda te: te.matmul(psBC2[:, 0:1], ones1[0:1, :], t11[0:1, 0:1],
                                          start=True, stop=True))
                c.se(lambda se: se.copy(c0b, psBC2[:, 0:1]))

                for qh in range(NSTRIP):
                    def ldq(sy, qh=qh):
                        out = []
                        for kc in range(KC):
                            out.append(sy.dma_start(
                                out=qh_s[:, kc * TS:(kc + 1) * TS],
                                in_=q_i[kc * 128:(kc + 1) * 128, qh * TS:(qh + 1) * TS]))
                        for kt in range(NKT):
                            out.append(sy.dma_start(
                                out=mask_s[:, kt * TS:(kt + 1) * TS],
                                in_=mask_d[kt * 128:(kt + 1) * 128, qh * TS:(qh + 1) * TS]))
                        return out
                    c.sy_dma(ldq, te=c.cnt["te"], se=c.cnt["se"], ve=c.cnt["ve"])

                    for h in range(NH0):
                        def ldkv(sy, h=h):
                            return [
                                sy.dma_start(out=kbuf[:, 0:T],
                                             in_=kf_h.ap()[h * 128:(h + 1) * 128, :]),
                                sy.dma_start(out=kbuf[:, T:2 * T],
                                             in_=kf_h.ap()[D + h * 128:D + (h + 1) * 128, :]),
                                sy.dma_start(out=vbuf.rearrange("p (kt d) -> p kt d", d=128),
                                             in_=vf_h.ap()[:, h * 128:(h + 1) * 128].rearrange(
                                                 "(kt p) d -> p kt d", p=128)),
                            ]
                        c.sy_dma(ldkv, te=c.cnt["te"], se=c.cnt["se"])

                        for kt in range(NKT):
                            eb = expb[kt % 2]
                            c.te(lambda te, h=h, kt=kt: te.matmul(
                                psS, kbuf[:, kt * 128:(kt + 1) * 128],
                                qh_s[:, h * TS:(h + 1) * TS],
                                start=True, stop=True), ld=c.cnt["ld"])
                            c.ve(lambda ve, kt=kt: ve.tensor_add(
                                psS, psS, mask_s[:, kt * TS:(kt + 1) * TS]))
                            c.se(lambda se, eb=eb: se.activation(
                                eb, psS, AF.Exp, bias=c0b, scale=1.0 / math.sqrt(HD)))
                            c.te(lambda te, kt=kt, eb=eb: te.matmul(
                                psR[0:1, :], ones128, eb,
                                start=(kt == 0), stop=(kt == NKT - 1)))
                            c.te(lambda te, kt=kt, eb=eb: te.matmul(
                                psPV, vbuf[:, kt * 128:(kt + 1) * 128], eb,
                                start=(kt == 0), stop=(kt == NKT - 1)))
                        c.ve(lambda ve: ve.reciprocal(rbc[0:1, :], psR[0:1, :]))
                        c.te(lambda te: te.matmul(psBC2, ones1[0:1, :], rbc[0:1, :],
                                                  start=True, stop=True))
                        c.se(lambda se: se.copy(rbc, psBC2))
                        c.ve(lambda ve, h=h: ve.tensor_mul(
                            attn_s[:, h * TS:(h + 1) * TS], psPV, rbc))

                    for oc in range(16):
                        c.sy_dma(lambda sy, oc=oc: [
                            sy.dma_start(out=wdn.rearrange("p (kc g) -> p kc g", g=128),
                                         in_=wden_d[:, oc * 128:(oc + 1) * 128].rearrange(
                                             "(kc p) g -> p kc g", p=128)),
                        ], te=c.cnt["te"])

                        def dmm(te):
                            for hc in range(KC):
                                ins = te.matmul(psD, wdn[:, hc * 128:(hc + 1) * 128],
                                                attn_s[:, hc * TS:(hc + 1) * TS],
                                                start=(hc == 0), stop=(hc == KC - 1))
                            return ins
                        c.te(dmm, ld=c.cnt["ld"])

                        def dacc(ve, oc=oc, qh=qh):
                            xsl = xacc[:, oc * T + qh * TS: oc * T + qh * TS + TS]
                            return ve.scalar_tensor_tensor(
                                xsl, psD, bdn[:, oc:oc + 1], xsl, ALU.add, ALU.add)
                        c.ve(dacc)

            with ExitStack() as es:
                sbm = lambda name, shape, dt=F32: es.enter_context(nc.sbuf_tensor(name, shape, dt))[:]
                psml = lambda name, shape, dt=F32: es.enter_context(nc.psum_tensor(name, shape, dt))[:]
                xn = sbm("p3xn", [128, KC * TS])
                sq = sbm("p3sq", [128, TS])
                xtmp = sbm("p3xtmp", [128, TS])
                mb = sbm("p3mb", [128, TS])
                rbb = sbm("p3rbb", [128, TS])
                stat = sbm("p3stat", [1, 8 * TS])
                wbuf = [sbm("p3wb0", [128, D]), sbm("p3wb1", [128, D])]
                otile = sbm("p3ot", [128, TS])
                xa = sbm("p3xa", [128, 2 * TS])
                hr = sbm("p3hr", [64, TS])
                lgt = sbm("p3lgt", [128, 4 * 16])
                coeff = sbm("p3coeff", [16, TS])
                co256 = sbm("p3co256", [128, 2 * TS])
                e16 = sbm("p3e16", [16, 256])
                rbias = sbm("p3rbias", [128, 4])
                bw = sbm("p3bw", [128, 16])
                wbf = sbm("p3wbf", [128, 2 * 128])
                t1 = sbm("p3t1", [128, 16])
                t2 = sbm("p3t2", [128, 16])
                t3 = sbm("p3t3", [128, 8])
                psA = psml("p3psA", [128, 512])
                psD = psml("p3psD", [128, 512])
                psR1 = psml("p3psR1", [1, 512])
                psR2 = psml("p3psR2", [1, 512])
                psBC = psml("p3psBC", [128, 512])
                psT = psml("p3psT", [128, 128])

                def ld30(sy):
                    out = [sy.dma_start(out=e16, in_=e16_d),
                           sy.dma_start(out=rbias[0:64, 0:1],
                                        in_=rb1_d.rearrange("(p one) -> p one", one=1)),
                           sy.dma_start(out=rbias[0:16, 1:2],
                                        in_=rb2_d.rearrange("(p one) -> p one", one=1)),
                           sy.dma_start(out=rbias[0:16, 2:3],
                                        in_=gates_d.rearrange("(p one) -> p one", one=1))]
                    for ocb in range(16):
                        out.append(sy.dma_start(
                            out=bw[:, ocb:ocb + 1],
                            in_=bias_d[ocb * 128:(ocb + 1) * 128].rearrange("(p one) -> p one", one=1)))
                    return out
                c.sy_dma(ld30)
                c.ve(lambda ve: ve.tensor_add(rbias[0:16, 1:2], rbias[0:16, 1:2],
                                              rbias[0:16, 2:3]), ld=c.cnt["ld"])

                for s in range(NSTRIP):
                    def xsl_(kc, s=s):
                        return xacc[:, kc * T + s * TS: kc * T + s * TS + TS]
                    for kc in range(KC):
                        c.se(lambda se, kc=kc: se.activation(sq, xsl_(kc), AF.Square),
                             ld=c.cnt["ld"])
                        c.te(lambda te, kc=kc: te.matmul(psR1[0:1, :], ones128, xsl_(kc),
                                                         start=(kc == 0), stop=(kc == KC - 1)))
                        c.te(lambda te, kc=kc: te.matmul(psR2[0:1, :], ones128, sq,
                                                         start=(kc == 0), stop=(kc == KC - 1)))
                    m = stat[0:1, 0:TS]; e2 = stat[0:1, TS:2 * TS]
                    var = stat[0:1, 2 * TS:3 * TS]; sd = stat[0:1, 3 * TS:4 * TS]
                    rstd = stat[0:1, 4 * TS:5 * TS]
                    c.ve(lambda ve: ve.tensor_scalar(m, psR1[0:1, :], 1.0 / D, None, ALU.mult))
                    c.ve(lambda ve: ve.tensor_scalar(e2, psR2[0:1, :], 1.0 / D, None, ALU.mult))
                    c.ve(lambda ve: ve.tensor_tensor(var, m, m, ALU.mult))
                    c.ve(lambda ve: ve.tensor_sub(var, e2, var))
                    c.se(lambda se: se.activation(sd, var, AF.Sqrt, bias=cpool[0:1, 0:1]))
                    c.ve(lambda ve: ve.reciprocal(rstd, sd))
                    c.te(lambda te: te.matmul(psBC, ones1[0:1, :], m, start=True, stop=True))
                    c.se(lambda se: se.copy(mb, psBC))
                    c.te(lambda te: te.matmul(psBC, ones1[0:1, :], rstd, start=True, stop=True))
                    c.se(lambda se: se.copy(rbb, psBC))
                    for kc in range(KC):
                        c.ve(lambda ve, kc=kc: ve.tensor_sub(xtmp, xsl_(kc), mb))
                        c.ve(lambda ve, kc=kc: ve.tensor_mul(xn[:, kc * TS:(kc + 1) * TS], xtmp, rbb))

                    def wload(sy, dram, col0, width, buf):
                        return [sy.dma_start(
                            out=buf[:, 0:KC * width].rearrange("p (kc g) -> p kc g", g=width),
                            in_=dram[:, col0:col0 + width].rearrange("(kc p) g -> p kc g", p=128))]

                    def projmm(te, buf, width):
                        for kc in range(KC):
                            ins = te.matmul(psA[0:width if width < 128 else 128, :],
                                            buf[:, kc * width:(kc + 1) * width],
                                            xn[:, kc * TS:(kc + 1) * TS],
                                            start=(kc == 0), stop=(kc == KC - 1))
                        return ins

                    for pc in range(PROJC + 1):
                        width = 64 if pc == PROJC else 128
                        col0 = pc * 128
                        buf = wbuf[pc % 2]
                        c.sy_dma(lambda sy, col0=col0, width=width, buf=buf:
                                 wload(sy, winT_d, col0, width, buf), te=c.cnt["te"])
                        c.te(lambda te, buf=buf, width=width: projmm(te, buf, width),
                             ld=c.cnt["ld"])
                        c.se(lambda se, width=width: se.copy(otile[0:width, :], psA[0:width, :]),
                             st=c.cnt["st"])
                        if pc < NZC:
                            dst, row = z_i, pc * 128
                        elif pc < PROJC:
                            dst, row = xbc_i, (pc - NZC) * 128
                        else:
                            dst, row = dtr_i, 0
                        c.ge_dma(lambda ge, dst=dst, row=row, width=width, s=s: ge.dma_start(
                            out=dst[row:row + width, s * TS:(s + 1) * TS],
                            in_=otile[0:width, :]))

                    for ocb in range(16):
                        buf = wbuf[ocb % 2]
                        c.sy_dma(lambda sy, ocb=ocb, buf=buf:
                                 wload(sy, wT_d, ocb * 128, 128, buf), te=c.cnt["te"])
                        c.te(lambda te, buf=buf: projmm(te, buf, 128), ld=c.cnt["ld"])
                        def bacc(ve, ocb=ocb, s=s):
                            xsl = xacc[:, ocb * T + s * TS: ocb * T + s * TS + TS]
                            return ve.scalar_tensor_tensor(
                                xsl, psA, bw[:, ocb:ocb + 1], xsl, ALU.add, ALU.add)
                        c.ve(bacc)

                    for ac in range(2):
                        buf = wbuf[ac % 2]
                        c.sy_dma(lambda sy, ac=ac, buf=buf:
                                 wload(sy, aT_d, ac * 128, 128, buf), te=c.cnt["te"])
                        c.te(lambda te, buf=buf: projmm(te, buf, 128), ld=c.cnt["ld"])
                        c.se(lambda se, ac=ac: se.copy(xa[:, ac * TS:(ac + 1) * TS], psA))

                    c.sy_dma(lambda sy: wload(sy, rw1T_d, 0, 64, wbuf[0]), te=c.cnt["te"])
                    c.te(lambda te: projmm(te, wbuf[0], 64), ld=c.cnt["ld"])
                    c.se(lambda se: se.activation(hr, psA[0:64, :], AF.Silu,
                                                  bias=rbias[0:64, 0:1]))
                    c.sy_dma(lambda sy: [sy.dma_start(out=wbf[0:64, 0:32], in_=rw2T_d)],
                             te=c.cnt["te"])
                    c.te(lambda te: te.matmul(psA[0:32, :], wbf[0:64, 0:32], hr,
                                              start=True, stop=True), ld=c.cnt["ld"])
                    c.se(lambda se: se.activation(otile[0:16, :], psA[0:16, :], AF.Identity,
                                                  bias=rbias[0:16, 1:2]))
                    for tt in range(4):
                        c.te(lambda te, tt=tt: te.transpose(
                            psT[:, 0:16], otile[0:16, tt * 128:(tt + 1) * 128],
                            eye[0:16, 0:16]))
                        c.se(lambda se, tt=tt: se.copy(lgt[:, tt * 16:(tt + 1) * 16],
                                                       psT[:, 0:16]))
                    for tt in range(4):
                        lt = lgt[:, tt * 16:(tt + 1) * 16]
                        m1 = t3[:, 0:1]; m2 = t3[:, 1:2]; dm = t3[:, 2:3]
                        e2c = t3[:, 3:4]; den = t3[:, 4:5]; w1 = t3[:, 5:6]; w2 = t3[:, 6:7]
                        c.ve(lambda ve, lt=lt: ve.reduce_max(m1, lt, axis=AX.X))
                        c.ve(lambda ve, lt=lt: ve.tensor_scalar(t1, lt, m1, None, ALU.is_ge))
                        c.ve(lambda ve, lt=lt: ve.scalar_tensor_tensor(
                            t2, t1, -1e30, lt, ALU.mult, ALU.add))
                        c.ve(lambda ve: ve.reduce_max(m2, t2, axis=AX.X))
                        c.ve(lambda ve: ve.tensor_scalar(t2, t2, m2, None, ALU.is_ge))
                        c.ve(lambda ve: ve.tensor_sub(dm, m2, m1))
                        c.se(lambda se: se.activation(e2c, dm, AF.Exp))
                        c.ve(lambda ve: ve.tensor_scalar(den, e2c, 1.0, None, ALU.add))
                        c.ve(lambda ve: ve.reciprocal(w1, den))
                        c.ve(lambda ve: ve.tensor_mul(w2, e2c, w1))
                        c.ve(lambda ve: ve.tensor_scalar(t1, t1, w1, None, ALU.mult))
                        c.ve(lambda ve: ve.tensor_scalar(t2, t2, w2, None, ALU.mult))
                        c.ve(lambda ve: ve.tensor_add(t1, t1, t2))
                        c.te(lambda te: te.transpose(psT[0:16, 0:128], t1, eye))
                        c.se(lambda se, tt=tt: se.copy(coeff[0:16, tt * 128:(tt + 1) * 128],
                                                       psT[0:16, 0:128]))
                    for ec in range(2):
                        c.te(lambda te, ec=ec: te.matmul(
                            psA, e16[:, ec * 128:(ec + 1) * 128], coeff[0:16, :],
                            start=True, stop=True))
                        c.se(lambda se, ec=ec: se.copy(co256[:, ec * TS:(ec + 1) * TS], psA))
                    for ec in range(2):
                        c.ve(lambda ve, ec=ec: ve.tensor_mul(
                            xa[:, ec * TS:(ec + 1) * TS], xa[:, ec * TS:(ec + 1) * TS],
                            co256[:, ec * TS:(ec + 1) * TS]))
                    for ocb in range(16):
                        c.sy_dma(lambda sy, ocb=ocb: [sy.dma_start(
                            out=wbf.rearrange("p (ec g) -> p ec g", g=128),
                            in_=bflat_d[:, ocb * 128:(ocb + 1) * 128].rearrange(
                                "(ec p) g -> p ec g", p=128))], te=c.cnt["te"])
                        def dmm3(te):
                            for ec in range(2):
                                ins = te.matmul(psD, wbf[:, ec * 128:(ec + 1) * 128],
                                                xa[:, ec * TS:(ec + 1) * TS],
                                                start=(ec == 0), stop=(ec == 1))
                            return ins
                        c.te(dmm3, ld=c.cnt["ld"])
                        def dacc3(ve, ocb=ocb, s=s):
                            xsl = xacc[:, ocb * T + s * TS: ocb * T + s * TS + TS]
                            return ve.scalar_tensor_tensor(
                                xsl, psD, 1.0, xsl, ALU.mult, ALU.add)
                        c.ve(dacc3)

                @blk.gpsimd
                def _(ge):
                    for k_ in ("te", "ve", "se"):
                        ge.wait_ge(c.sem[k_], c.cnt[k_])
                    for kc in range(KC):
                        c.i("st", ge.dma_start(out=xacc2_i[kc * 128:(kc + 1) * 128, :],
                                               in_=xacc[:, kc * T:(kc + 1) * T]), 16)
                    ge.wait_ge(c.sem["st"], c.cnt["st"])
                    c.i("st", ge.dma_start(out=xbtail_h.ap()[:, 0:3], in_=xbc_i[:, T - 3:T]), 16)
                    ge.wait_ge(c.sem["st"], c.cnt["st"])
                    ins_ = ge.collective_compute(
                        "AllGather", ALU.bypass, replica_groups=RG,
                        ins=[xbtail_h.ap().opt()], outs=[halo_h.ap().opt()])
                    c.i("cc", ins_, 1)
            cc_halo = c.cnt["cc"]
            esx.close()

            with ExitStack() as es:
                sbm = lambda name, shape, dt=F32: es.enter_context(nc.sbuf_tensor(name, shape, dt))[:]
                psml = lambda name, shape, dt=F32: es.enter_context(nc.psum_tensor(name, shape, dt))[:]
                ext = sbm("p4ext", [128, 3 + T])
                acc = sbm("p4acc", [128, T])
                xcc = sbm("p4xcc", [128, T])
                wc4 = sbm("p4wc4", [128, 4])
                cbc = sbm("p4cbc", [128, 1])
                dtb_s = sbm("p4dtb", [64, 1])
                a64 = sbm("p4a64", [64, 1])
                dtv = sbm("p4dtv", [64, T])
                tmp64 = sbm("p4tmp64", [64, T])
                tmp64b = sbm("p4tmp64b", [64, T])
                acs = sbm("p4acs", [64, T])
                cum = [sbm("p4cum0", [64, CH]), sbm("p4cum1", [64, CH])]
                f3 = sbm("p4f3", [64, T])
                dtt = sbm("p4dtt", [128, 8 * 64])
                f3t = sbm("p4f3t", [128, 8 * 64])
                x2t = sbm("p4x2t", [128, 128])
                x3t = sbm("p4x3t", [128, 128])
                xtt = sbm("p4xtt", [128, 128])
                btm = sbm("p4btm", [128, 2 * 128])
                xs2c = sbm("p4xs2c", [128, 2 * 4096])
                S = sbm("p4S", [128, 4096])
                erow = sbm("p4erow", [1, NCH * 64])
                prow = sbm("p4prow", [1, (NCH + 1) * 64])
                eab = sbm("p4eab", [128, 64])
                psT = psml("p4psT", [128, 128])
                psSt = psml("p4psSt", [128, 2048])
                psBC = psml("p4psBC", [128, 64])

                def ld40(sy):
                    return [
                        sy.dma_start(out=dtv, in_=dtr_i),
                        sy.dma_start(out=dtb_s, in_=dtb_d.rearrange("(p one) -> p one", one=1)),
                        sy.dma_start(out=a64, in_=alog_d.rearrange("(p one) -> p one", one=1)),
                    ]
                c.sy_dma(ld40, st=c.cnt["st"], cc=cc_halo)
                c.ve(lambda ve: ve.memset(S, 0.0))
                c.ve(lambda ve: ve.memset(prow[0:1, 0:64], 1.0))

                c.ve(lambda ve: ve.tensor_scalar(dtv, dtv, dtb_s, None, ALU.add),
                     ld=c.cnt["ld"])
                c.se(lambda se: se.activation(tmp64, dtv, AF.Abs))
                c.se(lambda se: se.activation(tmp64, tmp64, AF.Exp, scale=-1.0))
                c.se(lambda se: se.activation(tmp64, tmp64, AF.Ln, bias=cpool[0:64, 1:2]))
                c.ve(lambda ve: ve.tensor_relu(tmp64b, dtv))
                c.ve(lambda ve: ve.tensor_add(dtv, tmp64b, tmp64))
                c.ve(lambda ve: ve.tensor_scalar_min(dtv, dtv, 1e10))
                c.se(lambda se: se.activation(a64, a64, AF.Exp))
                c.ve(lambda ve: ve.tensor_scalar(a64, a64, -1.0, None, ALU.mult))
                c.ve(lambda ve: ve.tensor_scalar(tmp64, dtv, a64, None, ALU.mult))
                for ch_ in range(NCH):
                    c.ve(lambda ve, ch_=ch_: ve.tensor_copy(
                        cum[0], tmp64[:, ch_ * CH:(ch_ + 1) * CH]))
                    cur = 0
                    k = 1
                    while k < CH:
                        c.ve(lambda ve, cur=cur, k=k: ve.tensor_copy(
                            cum[1 - cur][0:64, 0:k], cum[cur][0:64, 0:k]))
                        c.ve(lambda ve, cur=cur, k=k: ve.tensor_add(
                            cum[1 - cur][0:64, k:CH], cum[cur][0:64, k:CH],
                            cum[cur][0:64, 0:CH - k]))
                        cur = 1 - cur
                        k *= 2
                    c.ve(lambda ve, ch_=ch_, cur=cur: ve.tensor_copy(
                        acs[:, ch_ * CH:(ch_ + 1) * CH], cum[cur]))
                c.ge_dma(lambda ge: ge.dma_start(out=acs_i, in_=acs[0:64, :]))
                for lt in range(8):
                    c.te(lambda te, lt=lt: te.transpose(
                        psT[:, 0:64], dtv[0:64, lt * 128:(lt + 1) * 128], eye[0:64, 0:64]))
                    c.se(lambda se, lt=lt: se.copy(dtt[:, lt * 64:(lt + 1) * 64], psT[:, 0:64]))
                for ch_ in range(NCH):
                    sl_ = slice(ch_ * CH, (ch_ + 1) * CH)
                    c.ve(lambda ve, ch_=ch_, sl_=sl_: ve.tensor_scalar(
                        f3[:, sl_], acs[0:64, sl_],
                        acs[0:64, ch_ * CH + CH - 1:ch_ * CH + CH], None, ALU.subtract))
                c.se(lambda se: se.activation(f3, f3[0:64, :], AF.Exp, scale=-1.0))
                c.ve(lambda ve: ve.tensor_mul(f3, dtv, f3))
                for lt in range(8):
                    c.te(lambda te, lt=lt: te.transpose(
                        psT[:, 0:64], f3[0:64, lt * 128:(lt + 1) * 128], eye[0:64, 0:64]))
                    c.se(lambda se, lt=lt: se.copy(f3t[:, lt * 64:(lt + 1) * 64], psT[:, 0:64]))
                for ch_ in range(NCH):
                    c.se(lambda se, ch_=ch_: se.activation(
                        a64, acs[0:64, ch_ * CH + CH - 1:ch_ * CH + CH], AF.Exp))
                    c.te(lambda te: te.transpose(psT[0:1, 0:64], a64, eye[0:64, 0:64]))
                    c.se(lambda se, ch_=ch_: se.copy(
                        erow[0:1, ch_ * 64:(ch_ + 1) * 64], psT[0:1, 0:64]))
                    c.ve(lambda ve, ch_=ch_: ve.tensor_mul(
                        prow[0:1, (ch_ + 1) * 64:(ch_ + 2) * 64],
                        prow[0:1, ch_ * 64:(ch_ + 1) * 64],
                        erow[0:1, ch_ * 64:(ch_ + 1) * 64]))
                c.ge_dma(lambda ge: ge.dma_start(out=pfac_i, in_=prow[0:1, 0:NCH * 64]))

                for ch_ in range(34):
                    def ldc(sy, ch_=ch_):
                        return [
                            sy.dma_start(out=ext[:, 3:3 + T],
                                         in_=xbc_i[ch_ * 128:(ch_ + 1) * 128, :]),
                            sy.dma_start(out=ext[:, 0:3],
                                         in_=halo_h.ap()[ch_ * 128:(ch_ + 1) * 128, 0:3]),
                            sy.dma_start(out=wc4, in_=cw_d[ch_ * 128:(ch_ + 1) * 128, :]),
                            sy.dma_start(out=cbc, in_=cb_d[ch_ * 128:(ch_ + 1) * 128]
                                         .rearrange("(p one) -> p one", one=1)),
                        ]
                    c.sy_dma(ldc, ve=c.cnt["ve"], se=c.cnt["se"], te=c.cnt["te"],
                             cc=cc_halo)
                    c.ve(lambda ve: ve.tensor_scalar(ext[:, 0:3], ext[:, 0:3], hm,
                                                     None, ALU.mult), ld=c.cnt["ld"])
                    c.ve(lambda ve: ve.tensor_scalar(acc, ext[:, 0:T], wc4[:, 0:1],
                                                     None, ALU.mult))
                    for j in range(1, 4):
                        c.ve(lambda ve, j=j: ve.scalar_tensor_tensor(
                            acc, ext[:, j:j + T], wc4[:, j:j + 1], acc, ALU.mult, ALU.add))
                    c.se(lambda se: se.activation(xcc, acc, AF.Silu, bias=cbc),
                         st=c.cnt["st"])
                    if ch_ == 32:
                        c.ge_dma(lambda ge: ge.dma_start(out=bfm_i, in_=xcc))
                    elif ch_ == 33:
                        c.ge_dma(lambda ge: ge.dma_start(out=cfm_i, in_=xcc))
                    else:
                        for lt in range(8):
                            c.te(lambda te, lt=lt: te.transpose(
                                psT, xcc[:, lt * 128:(lt + 1) * 128], eye),
                                st=c.cnt["st"])
                            c.se(lambda se: se.copy(xtt, psT), st=c.cnt["st"])
                            c.ge_dma(lambda ge, lt=lt, ch_=ch_: ge.dma_start(
                                out=xct_i[lt * 128:(lt + 1) * 128,
                                          ch_ * 128:(ch_ + 1) * 128], in_=xtt))
                            hb = (ch_ * 128) // 64
                            def x2(ve, lt, hb, ft, dst):
                                return ve.tensor_tensor(
                                    dst.rearrange("p (h w) -> p h w", w=64),
                                    xtt.rearrange("p (h w) -> p h w", w=64),
                                    ft[:, lt * 64:(lt + 1) * 64]
                                    .rearrange("p (h one) -> p h one", one=1)[:, hb:hb + 2, :]
                                    .broadcast_to([128, 2, 64]), ALU.mult)
                            c.ve(lambda ve, lt=lt, hb=hb: x2(ve, lt, hb, dtt, x2t),
                                 st=c.cnt["st"])
                            c.ge_dma(lambda ge, lt=lt, ch_=ch_: ge.dma_start(
                                out=xst_i[lt * 128:(lt + 1) * 128,
                                          ch_ * 128:(ch_ + 1) * 128], in_=x2t))
                            c.ve(lambda ve, lt=lt, hb=hb: x2(ve, lt, hb, f3t, x3t),
                                 st=c.cnt["st"])
                            c.ge_dma(lambda ge, lt=lt, ch_=ch_: ge.dma_start(
                                out=xsdt_i[lt * 128:(lt + 1) * 128,
                                           ch_ * 128:(ch_ + 1) * 128], in_=x3t))

                for ch_ in range(NCH):
                    def ldx2(sy, ch_=ch_):
                        out = []
                        for st_ in range(2):
                            out.append(sy.dma_start(
                                out=xs2c[:, st_ * 4096:(st_ + 1) * 4096],
                                in_=xsdt_i[ch_ * CH + st_ * 128: ch_ * CH + (st_ + 1) * 128, :]))
                        return out
                    c.sy_dma(ldx2, st=c.cnt["st"], ve=c.cnt["ve"], te=c.cnt["te"])
                    def ldb(sy, ch_=ch_):
                        return [sy.dma_start(out=xcc[:, 0:CH],
                                             in_=bfm_i[:, ch_ * CH:(ch_ + 1) * CH])]
                    c.sy_dma(ldb, st=c.cnt["st"], se=c.cnt["se"], te=c.cnt["te"])
                    for st_ in range(2):
                        c.te(lambda te, st_=st_: te.transpose(
                            psT, xcc[:, st_ * 128:(st_ + 1) * 128], eye), ld=c.cnt["ld"])
                        c.se(lambda se, st_=st_: se.copy(btm[:, st_ * 128:(st_ + 1) * 128], psT))
                    c.ge_dma(lambda ge, ch_=ch_: ge.dma_start(
                        out=sinl_i[ch_ * 128:(ch_ + 1) * 128, :], in_=S))
                    c.te(lambda te, ch_=ch_: te.matmul(
                        psBC, ones1[0:1, :], erow[0:1, ch_ * 64:(ch_ + 1) * 64],
                        start=True, stop=True))
                    c.se(lambda se: se.copy(eab, psBC))
                    for rnd in range(2):
                        def smm(te, rnd=rnd):
                            for st_ in range(2):
                                for sl in range(4):
                                    ins = te.matmul(
                                        psSt[:, sl * 512:(sl + 1) * 512],
                                        btm[:, st_ * 128:(st_ + 1) * 128],
                                        xs2c[:, st_ * 4096 + rnd * 2048 + sl * 512:
                                             st_ * 4096 + rnd * 2048 + (sl + 1) * 512],
                                        start=(st_ == 0), stop=(st_ == 1))
                            return ins
                        c.te(smm, ld=c.cnt["ld"], st=c.cnt["st"])
                        for sl in range(4):
                            off = rnd * 2048 + sl * 512
                            def smul(ve, off=off):
                                return ve.tensor_tensor(
                                    S[:, off:off + 512].rearrange("p (h w) -> p h w", w=64),
                                    S[:, off:off + 512].rearrange("p (h w) -> p h w", w=64),
                                    eab[:, (off // 64):(off // 64) + 8]
                                    .rearrange("p (h one) -> p h one", one=1)
                                    .broadcast_to([128, 8, 64]), ALU.mult)
                            c.ve(smul)
                            c.ve(lambda ve, off=off, sl=sl: ve.tensor_add(
                                S[:, off:off + 512], S[:, off:off + 512],
                                psSt[:, sl * 512:(sl + 1) * 512]))
                c.ge_dma(lambda ge: ge.dma_start(out=sloc_h.ap(), in_=S), ve=c.cnt["ve"])

                @blk.gpsimd
                def _(ge):
                    ge.wait_ge(c.sem["st"], c.cnt["st"])
                    ins_ = ge.collective_compute(
                        "AllGather", ALU.bypass, replica_groups=RG,
                        ins=[sloc_h.ap().opt()], outs=[sgth_h.ap().opt()])
                    c.i("cc", ins_, 1)
            cc_s = c.cnt["cc"]

            with ExitStack() as es:
                sbm = lambda name, shape, dt=F32: es.enter_context(nc.sbuf_tensor(name, shape, dt))[:]
                psml = lambda name, shape, dt=F32: es.enter_context(nc.psum_tensor(name, shape, dt))[:]
                xsc = sbm("p5xsc", [128, 2 * 4096])
                sinit = sbm("p5sinit", [128, 4096])
                stru = sbm("p5stru", [128, 4096])
                xcl = sbm("p5xcl", [128, 4096])
                bufA = sbm("p5bufA", [128, 4096])
                scr4 = sbm("p5scr4", [128, 4096])
                ybuf = sbm("p5ybuf", [128, 2 * 4096])
                wvo = sbm("p5wvo", [128, 16 * 128])
                bfmc = sbm("p5bfmc", [128, T])
                cfmc = sbm("p5cfmc", [128, T])
                acs_s = sbm("p5acs", [64, T])
                acsr = sbm("p5acsr", [1, 8 * CH])
                acst = sbm("p5acst", [128, 2 * 64])
                eat = sbm("p5eat", [128, 2 * 64])
                trib = sbm("p5trib", [128, 2 * CH])
                cbsm = sbm("p5cbsm", [128, 2 * CH])
                dbuf = sbm("p5dbuf", [128, 2 * CH])
                ebuf = sbm("p5ebuf", [128, 2 * CH])
                pf128 = sbm("p5pf128", [128, 64])
                d128 = sbm("p5d128", [128, 64])
                nw32 = sbm("p5nw32", [128, 32])
                dro = sbm("p5dro", [1, 64])
                pfr = sbm("p5pfr", [1, NCH * 64])
                tmp5 = sbm("p5tmp5", [128, 512])
                s1 = sbm("p5s1", [128, 1])
                ms = sbm("p5ms", [128, 1])
                rstd1 = sbm("p5rstd1", [128, 1])
                xaccl = sbm("p5xaccl", [128, 16 * 128])
                psY = [psml("p5psY0", [128, 512]), psml("p5psY1", [128, 512])]
                psY2 = [psml("p5psY2a", [128, 512]), psml("p5psY2b", [128, 512])]
                psCB = psml("p5psCB", [128, 512])
                psRB = psml("p5psRB", [128, CH])
                psT = psml("p5psT", [128, 128])
                psMISC = psml("p5psMISC", [128, 512])

                def ld50(sy):
                    return [
                        sy.dma_start(out=bfmc, in_=bfm_i),
                        sy.dma_start(out=cfmc, in_=cfm_i),
                        sy.dma_start(out=acs_s, in_=acs_i),
                        sy.dma_start(out=sinit, in_=sgth_h.ap()[0:128, :]),
                        sy.dma_start(out=pfr, in_=pfac_i),
                        sy.dma_start(out=dro, in_=dvec_d),
                        sy.dma_start(out=trib.rearrange("p (st l) -> p st l", l=CH),
                                     in_=tri_d.rearrange("(st p) l -> p st l", p=128)),
                    ]
                c.sy_dma(ld50, cc=cc_s)

                c.te(lambda te: te.matmul(psMISC[:, 0:64], ones1[0:1, :], dro[0:1, :],
                                          start=True, stop=True), ld=c.cnt["ld"])
                c.se(lambda se: se.copy(d128, psMISC[:, 0:64]))
                for ic in range(32):
                    c.sy_dma(lambda sy, ic=ic: [sy.dma_start(
                        out=nw32[:, ic:ic + 1],
                        in_=nw_d[ic * 128:(ic + 1) * 128].rearrange("(p one) -> p one", one=1))])

                for ch_ in range(NCH):
                    def ldc5(sy, ch_=ch_):
                        out = [sy.dma_start(out=xsc[:, st_ * 4096:(st_ + 1) * 4096],
                                            in_=xst_i[ch_ * CH + st_ * 128:
                                                      ch_ * CH + (st_ + 1) * 128, :])
                               for st_ in range(2)]
                        out.append(sy.dma_start(out=stru, in_=sinl_i[ch_ * 128:(ch_ + 1) * 128, :]))
                        return out
                    c.sy_dma(ldc5, te=c.cnt["te"], ve=c.cnt["ve"], se=c.cnt["se"])

                    c.te(lambda te, ch_=ch_: te.matmul(
                        psMISC[:, 0:64], ones1[0:1, :], pfr[0:1, ch_ * 64:(ch_ + 1) * 64],
                        start=True, stop=True), ld=c.cnt["ld"])
                    c.se(lambda se: se.copy(pf128, psMISC[:, 0:64]))
                    c.ve(lambda ve: ve.tensor_scalar(pf128, pf128, hm, None, ALU.mult))
                    for sl in range(8):
                        off = sl * 512
                        def st1(ve, off=off):
                            return ve.tensor_tensor(
                                tmp5.rearrange("p (h w) -> p h w", w=64),
                                sinit[:, off:off + 512].rearrange("p (h w) -> p h w", w=64),
                                pf128[:, off // 64:off // 64 + 8]
                                .rearrange("p (h one) -> p h one", one=1)
                                .broadcast_to([128, 8, 64]), ALU.mult)
                        c.ve(st1, ld=c.cnt["ld"])
                        c.ve(lambda ve, off=off: ve.tensor_add(
                            stru[:, off:off + 512], stru[:, off:off + 512], tmp5))

                    for st_ in range(2):
                        c.te(lambda te, ch_=ch_, st_=st_: te.transpose(
                            psT[:, 0:64],
                            acs_s[0:64, ch_ * CH + st_ * 128: ch_ * CH + (st_ + 1) * 128],
                            eye[0:64, 0:64]))
                        c.se(lambda se, st_=st_: se.copy(acst[:, st_ * 64:(st_ + 1) * 64],
                                                         psT[:, 0:64]))
                        c.se(lambda se, st_=st_: se.activation(
                            eat[:, st_ * 64:(st_ + 1) * 64], psT[:, 0:64], AF.Exp))
                    for st_ in range(2):
                        c.te(lambda te, ch_=ch_, st_=st_: te.matmul(
                            psCB[:, st_ * CH:(st_ + 1) * CH],
                            bfmc[:, ch_ * CH + st_ * 128: ch_ * CH + (st_ + 1) * 128],
                            cfmc[:, ch_ * CH:(ch_ + 1) * CH],
                            start=True, stop=True))
                    c.ve(lambda ve: ve.tensor_mul(cbsm, psCB, trib))

                    for hg in range(8):
                        def ldacs(sy, hg=hg, ch_=ch_):
                            return [sy.dma_start(
                                out=acsr[0:1, h8_ * CH:(h8_ + 1) * CH],
                                in_=acs_i[hg * 8 + h8_, ch_ * CH:(ch_ + 1) * CH]
                                .rearrange("(one l) -> one l", one=1)) for h8_ in range(8)]
                        c.sy_dma(ldacs, te=c.cnt["te"])
                        for h8 in range(8):
                            h = hg * 8 + h8
                            c.te(lambda te, h8=h8: te.matmul(
                                psRB, ones1[0:1, :], acsr[0:1, h8 * CH:(h8 + 1) * CH],
                                start=True, stop=True), ld=c.cnt["ld"])
                            for st_ in range(2):
                                c.ve(lambda ve, st_=st_, h=h: ve.tensor_scalar(
                                    dbuf[:, st_ * CH:(st_ + 1) * CH], psRB,
                                    acst[:, st_ * 64 + h:st_ * 64 + h + 1], 0.0,
                                    ALU.subtract, ALU.min))
                            c.se(lambda se: se.activation(ebuf, dbuf, AF.Exp))
                            c.ve(lambda ve: ve.tensor_mul(ebuf, ebuf, cbsm))
                            for lt in range(2):
                                def ymm(te, h=h, h8=h8, lt=lt, ch_=ch_):
                                    te.matmul(psY2[lt][:, h8 * 64:(h8 + 1) * 64],
                                              cfmc[:, ch_ * CH + lt * 128:
                                                   ch_ * CH + (lt + 1) * 128],
                                              stru[:, h * 64:(h + 1) * 64],
                                              start=True, stop=True)
                                    for st_ in range(2):
                                        ins = te.matmul(
                                            psY[lt][:, h8 * 64:(h8 + 1) * 64],
                                            ebuf[:, st_ * CH + lt * 128:
                                                 st_ * CH + (lt + 1) * 128],
                                            xsc[:, st_ * 4096 + h * 64:
                                                st_ * 4096 + (h + 1) * 64],
                                            start=(st_ == 0), stop=(st_ == 1))
                                    return ins
                                c.te(ymm)
                        for lt in range(2):
                            def yevac(ve, lt=lt, hg=hg):
                                return ve.tensor_tensor(
                                    ybuf[:, lt * 4096 + hg * 512: lt * 4096 + (hg + 1) * 512]
                                    .rearrange("p (h w) -> p h w", w=64),
                                    psY2[lt][:, :].rearrange("p (h w) -> p h w", w=64),
                                    eat[:, lt * 64 + hg * 8: lt * 64 + (hg + 1) * 8]
                                    .rearrange("p (h one) -> p h one", one=1)
                                    .broadcast_to([128, 8, 64]), ALU.mult)
                            c.ve(yevac)
                            c.ve(lambda ve, lt=lt, hg=hg: ve.tensor_add(
                                ybuf[:, lt * 4096 + hg * 512: lt * 4096 + (hg + 1) * 512],
                                ybuf[:, lt * 4096 + hg * 512: lt * 4096 + (hg + 1) * 512],
                                psY[lt][:, :]))

                    for lt in range(2):
                        tok0 = ch_ * CH + lt * 128
                        def ldz(sy, tok0=tok0):
                            return [sy.dma_start(
                                out=scr4.rearrange("p (zc t) -> p zc t", t=128),
                                in_=z_i[:, tok0:tok0 + 128].rearrange(
                                    "(zc p) t -> p zc t", p=128))]
                        c.sy_dma(lambda sy, tok0=tok0: [
                            sy.dma_start(out=xcl, in_=xct_i[tok0:tok0 + 128, :])],
                            te=c.cnt["te"], ve=c.cnt["ve"], se=c.cnt["se"], st=c.cnt["st"])
                        c.sy_dma(ldz, te=c.cnt["te"], ve=c.cnt["ve"], se=c.cnt["se"])
                        for zc in range(32):
                            c.te(lambda te, zc=zc: te.transpose(
                                psT, scr4[:, zc * 128:(zc + 1) * 128], eye), ld=c.cnt["ld"])
                            c.se(lambda se, zc=zc: se.copy(
                                bufA[:, zc * 128:(zc + 1) * 128], psT))
                        ysl = ybuf[:, lt * 4096:(lt + 1) * 4096]
                        def xcd(ve):
                            return ve.tensor_tensor(
                                xcl.rearrange("p (h w) -> p h w", w=64),
                                xcl.rearrange("p (h w) -> p h w", w=64),
                                d128.rearrange("p (h one) -> p h one", one=1)
                                .broadcast_to([128, 64, 64]), ALU.mult)
                        c.ve(xcd, ld=c.cnt["ld"])
                        c.ve(lambda ve, ysl=ysl: ve.tensor_add(ysl, ysl, xcl))
                        c.se(lambda se: se.activation(scr4, bufA, AF.Silu))
                        c.ve(lambda ve, ysl=ysl: ve.tensor_mul(ysl, ysl, scr4))
                        c.se(lambda se, ysl=ysl: se.activation(scr4, ysl, AF.Square,
                                                               accum_out=s1))
                        c.ve(lambda ve: ve.tensor_scalar(ms, s1, 1.0 / 4096.0, None, ALU.mult))
                        c.se(lambda se: se.activation(ms, ms, AF.Sqrt, bias=cpool[:, 0:1]))
                        c.ve(lambda ve: ve.reciprocal(rstd1, ms))
                        c.ve(lambda ve, ysl=ysl: ve.tensor_scalar(ysl, ysl, rstd1,
                                                                  None, ALU.mult))
                        for ic in range(32):
                            c.te(lambda te, ysl=ysl, ic=ic: te.transpose(
                                psT, ysl[:, ic * 128:(ic + 1) * 128], eye))
                            c.ve(lambda ve, ic=ic: ve.tensor_scalar(
                                bufA[:, ic * 128:(ic + 1) * 128], psT,
                                nw32[:, ic:ic + 1], None, ALU.mult))
                        def ldxa(sy, tok0=tok0):
                            out = []
                            for oc in range(16):
                                out.append(sy.dma_start(
                                    out=xaccl[:, oc * 128:(oc + 1) * 128],
                                    in_=xacc2_i[oc * 128:(oc + 1) * 128, tok0:tok0 + 128]))
                            return out
                        c.sy_dma(ldxa, st=c.cnt["st"], ve=c.cnt["ve"])
                        for oc in range(16):
                            for half in range(2):
                                c.sy_dma(lambda sy, oc=oc, half=half: [sy.dma_start(
                                    out=wvo.rearrange("p (ic g) -> p ic g", g=128),
                                    in_=woutT_d[half * 2048:(half + 1) * 2048,
                                                oc * 128:(oc + 1) * 128].rearrange(
                                        "(ic p) g -> p ic g", p=128))], te=c.cnt["te"])
                                def wmm(te, half=half):
                                    for ic in range(16):
                                        ins = te.matmul(
                                            psMISC[:, 0:128], wvo[:, ic * 128:(ic + 1) * 128],
                                            bufA[:, half * 2048 + ic * 128:
                                                 half * 2048 + (ic + 1) * 128],
                                            start=(half == 0 and ic == 0),
                                            stop=(half == 1 and ic == 15))
                                    return ins
                                c.te(wmm, ld=c.cnt["ld"])
                            c.ve(lambda ve, oc=oc: ve.scalar_tensor_tensor(
                                xaccl[:, oc * 128:(oc + 1) * 128], psMISC[:, 0:128],
                                1.0, xaccl[:, oc * 128:(oc + 1) * 128],
                                ALU.mult, ALU.add))
                        def stx(ge, tok0=tok0):
                            return ge.dma_start(
                                out=out_o.rearrange("(oc p) t -> p oc t", p=128)
                                [:, :, tok0:tok0 + 128],
                                in_=xaccl.rearrange("p (oc g) -> p oc g", g=128))
                        c.ge_dma(stx, ve=c.cnt["ve"])

            @blk.gpsimd
            def _(ge):
                ge.wait_ge(c.sem["st"], c.cnt["st"])
    return nc




def run_fused(inputs):
    x0 = np.asarray(inputs["L0_attn_hidden_states"])
    wqkv = np.ascontiguousarray(np.asarray(inputs["L0_attn_W_qkv"], np.float32))
    bqkv = np.ascontiguousarray(np.asarray(inputs["L0_attn_b_qkv"], np.float32))
    gwT = np.ascontiguousarray(np.asarray(inputs["L0_ffn_gate_w"], np.float32).T)
    uwT = np.ascontiguousarray(np.asarray(inputs["L0_ffn_up_w"], np.float32).T)
    dwT = np.ascontiguousarray(np.asarray(inputs["L0_ffn_down_w"], np.float32).T)
    gs = np.ascontiguousarray(np.asarray(inputs["L0_ffn_gate_s"], np.float32))
    us = np.ascontiguousarray(np.asarray(inputs["L0_ffn_up_s"], np.float32))
    ds = np.ascontiguousarray(np.asarray(inputs["L0_ffn_down_s"], np.float32))
    wden = np.ascontiguousarray(np.asarray(inputs["L0_attn_W_dense"], np.float32))
    bden = np.ascontiguousarray(np.asarray(inputs["L0_attn_b_dense"], np.float32))
    winT = np.ascontiguousarray(np.asarray(inputs["L1_attn_W_in"], np.float32).T)
    wT = np.ascontiguousarray(np.asarray(inputs["L1_ffn_W"], np.float32).T)
    aT = np.ascontiguousarray(np.asarray(inputs["L1_ffn_A"], np.float32).reshape(256, D).T)
    rw1T = np.ascontiguousarray(np.asarray(inputs["L1_ffn_rw1"], np.float32).T)
    rw2T = np.ascontiguousarray(np.asarray(inputs["L1_ffn_rw2"], np.float32).T)
    rb1 = np.ascontiguousarray(np.asarray(inputs["L1_ffn_rb1"], np.float32))
    rb2 = np.ascontiguousarray(np.asarray(inputs["L1_ffn_rb2"], np.float32)[:16])
    gates = np.ascontiguousarray(np.asarray(inputs["L1_ffn_gates"], np.float32))
    biasW = np.ascontiguousarray(np.asarray(inputs["L1_ffn_bias"], np.float32))
    bflat = np.ascontiguousarray(
        np.asarray(inputs["L1_ffn_B"], np.float32).transpose(0, 2, 1).reshape(256, D))
    cw = np.ascontiguousarray(np.asarray(inputs["L1_attn_conv_w"], np.float32).reshape(4352, 4))
    cb = np.ascontiguousarray(np.asarray(inputs["L1_attn_conv_b"], np.float32))
    dtb = np.ascontiguousarray(np.asarray(inputs["L1_attn_dt_bias"], np.float32))
    alog = np.ascontiguousarray(np.asarray(inputs["L1_attn_A_log"], np.float32))
    dvec = np.ascontiguousarray(np.asarray(inputs["L1_attn_D"], np.float32)[None, :])
    nw = np.ascontiguousarray(np.asarray(inputs["L1_attn_norm_w"], np.float32))
    woutT = np.ascontiguousarray(np.asarray(inputs["L1_attn_W_out"], np.float32).T)
    e16 = np.zeros((16, 256), np.float32)
    for e in range(16):
        e16[e, e * 16:(e + 1) * 16] = 1.0
    eye = np.eye(128, dtype=np.float32)
    p32 = _p32_lhsT()
    ss, ll = np.meshgrid(np.arange(CH), np.arange(CH), indexing="ij")
    tri = (ss <= ll).astype(np.float32)
    nc = build_fused()
    in_maps = []
    for core in range(NCORES):
        b, hf = core // 2, core % 2
        t0 = hf * T
        cosx, sinx = _rope_tables(t0, T)
        x0T = np.ascontiguousarray(x0[b, t0:t0 + T, :].T)
        kk = np.arange(2 * T)[:, None]
        qq = np.arange(T)[None, :]
        mask = np.where(kk <= t0 + qq, 0.0, -1e30).astype(np.float32)
        hm = np.full((128, 1), 1.0 if hf == 1 else 0.0, np.float32)
        in_maps.append(dict(
            x0=x0T, wqkv=wqkv, bqkv=bqkv, gwT=gwT, uwT=uwT, dwT=dwT, gs=gs, us=us,
            ds=ds, cosx=cosx, sinx=sinx, p32=p32, eye=eye, mask=mask, wden=wden,
            bden=bden, winT=winT, wT=wT, aT=aT, rw1T=rw1T, rw2T=rw2T, rb1=rb1,
            rb2=rb2, gates=gates, biasW=biasW, bflat=bflat, e16=e16, cw=cw, cb=cb,
            dtb=dtb, alog=alog, dvec=dvec, nw=nw, woutT=woutT, tri=tri, hm=hm))
    res = run_bass_kernel_spmd(nc, in_maps, core_ids=list(range(NCORES)))
    out = np.empty_like(x0)
    for core in range(NCORES):
        b, hf = core // 2, core % 2
        t0 = hf * T
        out[b, t0:t0 + T, :] = res.results[core]["out_o"].T
    return out


KERNEL_MODE = "fused"


def kernel(**inputs):
    if KERNEL_MODE == "fused":
        return run_fused(inputs)
    l0 = run_l0(inputs)
    l2 = run_l2(inputs, l0)
    x1_cores = [l2[c]["x1_o"] for c in range(NCORES)]
    l3a = run_l3a(inputs, x1_cores)
    l3b = run_l3b(inputs, l3a)
    l4 = run_l4(inputs, l3a, l3b)
    x0 = np.asarray(inputs["L0_attn_hidden_states"])
    out = np.empty_like(x0)
    for core in range(NCORES):
        b, hf = core // 2, core % 2
        t0 = hf * T
        out[b, t0:t0 + T, :] = l4[core]["out_o"].T
    return out
